# version 06
# speedup vs baseline: 1.3432x; 1.0072x over previous
import math
from collections import defaultdict

import numpy as np
import ml_dtypes

import concourse.bass as bass
import concourse.mybir as mybir
from concourse.bass_utils import run_bass_kernel_spmd
from concourse.alu_op_type import AluOpType

F32 = mybir.dt.float32
BF16 = mybir.dt.bfloat16
U16 = mybir.dt.uint16
AF = mybir.ActivationFunctionType
OP = AluOpType

BF = ml_dtypes.bfloat16

S = 4096
E = 1024
H = 16
D = 64
HP = 8
NTT = S // 128
NTG = S // 512
NEXP = 8
ENGS = ["tensor", "vector", "scalar", "gpsimd", "sync"]
QUEUE = {"tensor": "tensor", "vector": "vector", "scalar": "scalar",
         "gpsimd": "gpsimd", "sync": "sync"}


class Plan:

    def __init__(self, nc):
        self.nc = nc
        self.steps = {e: [] for e in ENGS}
        self.count = defaultdict(int)
        self.dcount = defaultdict(int)
        self.sems = {}
        self.fence = []
        self.fence_epoch = 0
        self.fence_seen = {}

    def sem(self, name):
        if name not in self.sems:
            self.sems[name] = self.nc.semaphore(name).__enter__()
        return self.sems[name]

    def emit(self, eng, fn, waits=(), inc=None):
        for n, v in waits:
            self.sem(n)
        s = [(self.sems[n], v) for (n, v) in waits if v > 0]
        incsem = self.sem(inc) if inc else None

        def step(h, fn=fn, s=s, incsem=incsem):
            for sem, v in s:
                h.wait_ge(sem, v)
            inst = fn(h)
            if incsem is not None:
                inst.then_inc(incsem, 1)
        self.steps[eng].append(step)
        if inc:
            self.count[inc] += 1
            return self.count[inc]
        return None

    def dma(self, eng, out, in_, stream, waits=()):
        waits = list(waits)
        if stream not in self.NOFENCE and \
                self.fence_seen.get(stream) != self.fence_epoch:
            self.fence_seen[stream] = self.fence_epoch
            waits = waits + [w for w in self.fence if w[0] != stream]
        for n, v in waits:
            self.sem(n)
        s = [(self.sems[n], v) for (n, v) in waits if v > 0]
        sem = self.sem(stream)

        def step(h, out=out, in_=in_, s=s, sem=sem):
            for ss, v in s:
                h.wait_ge(ss, v)
            h.dma_start(out=out, in_=in_).then_inc(sem, 16)
        self.steps[eng].append(step)
        self.dcount[stream] += 16
        return self.dcount[stream]

    def w(self, name):
        if name in self.dcount:
            return self.dcount[name]
        return self.count[name]

    def init_psum(self, banks, eye, const_dma):
        first = True
        for b in banks:
            for j in range(4):
                w = [(const_dma, self.dcount[const_dma])] if first else []
                first = False
                self.emit("tensor", lambda h, b=b, j=j: h.matmul(
                    out=b[:, j * 128:(j + 1) * 128], lhsT=eye[:, :],
                    rhs=eye[:, :], start=True, stop=True),
                    waits=w, inc="pe")

    NOFENCE = ("gci0", "gci1", "gco0", "gco1")

    def barrier(self, extra_dma=None):
        names = ["pe", "dve", "act"]
        waits = [(n, self.count[n]) for n in names]
        waits += [(s, v) for s, v in self.dcount.items()
                  if s not in self.NOFENCE]
        for eng in ["tensor", "vector", "scalar"]:
            me = {"tensor": "pe", "vector": "dve", "scalar": "act"}[eng]
            ww = [(n, v) for (n, v) in waits if n != me and v > 0]
            if ww:
                self.emit(eng, lambda h: h.nop(), waits=ww)
        self.fence = [(s, v) for s, v in self.dcount.items()
                      if v > 0 and s not in self.NOFENCE]
        self.fence += [(n, self.count[n]) for n in names if self.count[n] > 0]
        self.fence_epoch += 1

    def finalize(self):
        waits = [(n, v) for n, v in self.dcount.items() if v > 0]
        waits += [(n, v) for n, v in self.count.items() if v > 0]
        self.emit("gpsimd", lambda h: h.nop(), waits=waits)
        with self.nc.Block() as block:
            for e in ENGS:
                def run(h, e=e):
                    for step in self.steps[e]:
                        step(h)
                getattr(block, e)(run)



def host_tables():
    t = {}
    d64 = np.float64
    a = 1.0 / 10000.0 ** np.linspace(0.0, 1.0, D // 2, dtype=d64)
    angle = np.repeat(a, 2)
    idx = np.arange(S, dtype=d64)
    sin = np.sin(idx[:, None] * angle[None, :])
    cos = np.cos(idx[:, None] * angle[None, :])
    decay = np.log(1.0 - 2.0 ** (-5.0 - np.arange(H, dtype=d64)))

    cosq = np.zeros((HP, 128, S), d64)
    sinq = np.zeros((HP, 128, S), d64)
    cosk = np.zeros((HP, 128, S), d64)
    sink = np.zeros((HP, 128, S), d64)
    scalek = 1.0 / math.sqrt(E)
    for p in range(HP):
        for x in range(2):
            h = 2 * p + x
            alpha = np.exp(decay[h] * (idx % 512))
            beta = np.exp(-decay[h] * (idx % 128))
            rows = slice(64 * x, 64 * x + 64)
            cosq[p, rows, :] = cos.T * alpha[None, :]
            sinq[p, rows, :] = sin.T * alpha[None, :]
            cosk[p, rows, :] = cos.T * beta[None, :] * scalek
            sink[p, rows, :] = sin.T * beta[None, :] * scalek
    t["cosq"] = cosq.astype(BF)
    t["sinq"] = sinq.astype(BF)
    t["cosk"] = cosk.astype(BF)
    t["sink"] = sink.astype(BF)

    lam = np.exp(decay)
    c = np.cumsum(np.exp(decay[:, None] * idx[None, :]), axis=1)
    r = 1.0 / np.sqrt(c)
    rcol = np.zeros((HP, NTG, 128, 8), np.float32)
    for p in range(HP):
        for qg in range(NTG):
            for sb in range(4):
                i0 = qg * 512 + sb * 128
                rcol[p, qg, :, sb * 2 + 0] = r[2 * p, i0:i0 + 128]
                rcol[p, qg, :, sb * 2 + 1] = r[2 * p + 1, i0:i0 + 128]
    t["rcol"] = rcol

    gam = np.zeros((H, NTG, NTG * 4), d64)
    for h in range(H):
        for qg in range(NTG):
            for kc in range(4 * (qg + 1)):
                gam[h, qg, kc] = math.exp(decay[h] * (qg * 512 - kc * 128))
    t["gamma"] = gam

    eye = np.eye(128, dtype=np.float32)
    t["eyeb"] = eye.astype(BF)
    t["eyef"] = eye
    RT = np.zeros((128, 128), np.float32)
    for tt in range(64):
        RT[2 * tt + 1, 2 * tt] = -1.0
        RT[2 * tt, 2 * tt + 1] = 1.0
    t["rotT"] = RT.astype(BF)
    T01 = np.triu(np.ones((128, 128), np.float32))
    t["tril"] = T01.astype(BF)
    ones_col = np.ones((128, 1), np.float32)
    t["ones_col"] = ones_col.astype(BF)
    oA = np.zeros((128, 1), np.float32); oA[0:64] = 1.0
    oB = np.zeros((128, 1), np.float32); oB[64:128] = 1.0
    t["onesA_col"] = oA.astype(BF)
    t["onesB_col"] = oB.astype(BF)
    t["ones_row"] = np.ones((1, 128), np.float32).astype(BF)
    oAr = np.zeros((1, 128), np.float32); oAr[0, 0:64] = 1.0
    oBr = np.zeros((1, 128), np.float32); oBr[0, 64:128] = 1.0
    t["onesA_row"] = oAr.astype(BF)
    t["onesB_row"] = oBr.astype(BF)
    mix = np.zeros((128, 256), np.float32)
    mix[64, 0:64] = 1.0
    mix[0, 128 + 64:128 + 128] = 1.0
    t["mixrow"] = mix.astype(BF)
    t["mixrowf"] = mix
    bc = np.zeros((128, 128), np.float32)
    bc[64, 0:64] = 1.0
    t["bc128"] = bc.astype(BF)
    return t


TAB = None


def _alloc(nc, bag, name, shape, dt):
    cm = nc.sbuf_tensor(name, shape, dt, side="right")
    h = cm.__enter__()
    bag.append(cm)
    return h


def _free(bag):
    for cm in reversed(bag):
        cm.__exit__(None, None, None)
    bag.clear()


class ConstDB:

    def __init__(self, nc, P, f32_vals, bf16_vals=()):
        f32_vals = sorted(set(float(v) for v in f32_vals))
        bf16_vals = sorted(set(float(v) for v in bf16_vals))
        n1 = max(1, len(f32_vals))
        n2 = max(1, len(bf16_vals))
        self.d1 = nc.dram_tensor("cdbf32", [128, n1], F32, kind="ExternalInput").ap()
        self.s1 = nc.sbuf_tensor("cdbf32_s", [128, n1], F32).__enter__()
        self.d2 = nc.dram_tensor("cdbbf16", [128, n2], BF16, kind="ExternalInput").ap()
        self.s2 = nc.sbuf_tensor("cdbbf16_s", [128, n2], BF16).__enter__()
        P.dma("sync", self.s1[:, :], self.d1, "const")
        P.dma("sync", self.s2[:, :], self.d2, "const")
        for i, v in enumerate(f32_vals):
            nc.const_aps.register(F32, v, self.s1[:, i:i + 1])
        for i, v in enumerate(bf16_vals):
            nc.const_aps.register(BF16, v, self.s2[:, i:i + 1])


def _l0_const_vals():
    gam = TAB["gamma"]
    vals = [1e-5]
    for p in range(HP):
        for qg in range(NTG):
            for kc in range(4 * (qg + 1)):
                g = float(gam[2 * p, qg, kc])
                if g >= GAMMA_SKIP:
                    vals.append(g)
    return vals


def _cdb_host_arrays(f32_vals, bf16_vals=()):
    f32_vals = sorted(set(float(v) for v in f32_vals))
    bf16_vals = sorted(set(float(v) for v in bf16_vals))
    a1 = np.tile(np.array(f32_vals or [0.0], np.float32), (128, 1))
    a2 = np.tile(np.array(bf16_vals or [0.0], BF), (128, 1))
    return a1, a2


def _consts(nc, P):
    c = {}
    for name, shape, dt in [
        ("eyeb", [128, 128], BF16), ("eyef", [128, 128], F32),
        ("rotT", [128, 128], BF16),
        ("tril", [128, 128], BF16), ("ones_col", [128, 1], BF16),
        ("onesA_col", [128, 1], BF16), ("onesB_col", [128, 1], BF16),
        ("ones_row", [1, 128], BF16),
        ("onesA_row", [1, 128], BF16), ("onesB_row", [1, 128], BF16),
        ("mixrow", [128, 256], BF16), ("mixrowf", [128, 256], F32),
        ("bc128", [128, 128], BF16),
    ]:
        d = nc.dram_tensor(name, shape, dt, kind="ExternalInput").ap()
        sb = nc.sbuf_tensor(name + "_s", shape, dt).__enter__()
        P.dma("sync", sb[:, :], d, "const")
        c[name] = sb
    return c


def emit_cast_stream(nc, P, src_dram, n_rows, n_cols, eng, dst_sbuf_tiles=None,
                     dst_scratch=None, pfx="c", stage=None, stage_bufs=None):
    esem = {"vector": "dve", "scalar": "act", "gpsimd": "gps"}[eng]
    nt = n_rows // 128
    own_stage = stage is None
    if own_stage:
        stage = [nc.sbuf_tensor(f"{pfx}_st{i}", [128, n_cols], F32).__enter__()
                 for i in range(2)]
    if dst_scratch is not None and stage_bufs is None:
        stage_bufs = [nc.sbuf_tensor(f"{pfx}_bb{i}", [128, n_cols], BF16).__enter__()
                      for i in range(2)]
    last_cast = {}
    last_out = {}
    for t in range(nt):
        sl = t % 2
        w = [(esem, last_cast.get(sl, 0))]
        P.dma("sync", stage[sl][:, :], src_dram[t * 128:(t + 1) * 128, :],
              f"{pfx}_in{sl}", waits=w)
        if dst_sbuf_tiles is not None:
            dst = dst_sbuf_tiles[t][:, :]
        else:
            dst = stage_bufs[sl][:, :]
        ww = [(f"{pfx}_in{sl}", P.dcount[f"{pfx}_in{sl}"])]
        if dst_scratch is not None:
            ww.append((f"{pfx}_out{sl}", P.dcount.get(f"{pfx}_out{sl}", 0)))
        if eng == "scalar":
            cc = P.emit(eng, lambda h, dst=dst, sl=sl: h.copy(out=dst, in_=stage[sl][:, :]),
                        waits=ww, inc=esem)
        else:
            cc = P.emit(eng, lambda h, dst=dst, sl=sl: h.tensor_copy(out=dst, in_=stage[sl][:, :]),
                        waits=ww, inc=esem)
        last_cast[sl] = cc
        if dst_scratch is not None:
            P.dma("sync", dst_scratch[t * 128:(t + 1) * 128, :], stage_bufs[sl][:, :],
                  f"{pfx}_out{sl}", waits=[(esem, cc)])
            last_out[sl] = P.dcount[f"{pfx}_out{sl}"]


def emit_ln_feature(nc, P, c, x_dram, xn_tiles, banks, pfx, eps=1e-5,
                    bag=None):
    if bag is None:
        bag = []
    xs = [_alloc(nc, bag, f"{pfx}_xs{i}", [128, E], F32) for i in range(2)]
    xt = [_alloc(nc, bag, f"{pfx}_xt{i}", [128, E], BF16) for i in range(2)]
    sqd = [_alloc(nc, bag, f"{pfx}_sq{i}", [128, E], F32) for i in range(2)]
    cols = _alloc(nc, bag, f"{pfx}_cols", [128, 16], F32)
    bk_a, bk_b = banks
    bva = bk_a[:, :].bitcast(BF16)
    bvb = bk_b[:, :].bitcast(BF16)
    prev = {"aff": {}, "tp": {}, "evac": {}, "sq": {}}
    for t in range(NTT):
        sl = t % 2
        w = []
        if t >= 2:
            w += [("dve", prev["aff"][t - 2]), ("act", prev["sq"][t - 2])]
        P.dma("sync", xs[sl][:, :], x_dram[t * 128:(t + 1) * 128, :],
              f"{pfx}_x{sl}", waits=w)
        dsem = (f"{pfx}_x{sl}", P.dcount[f"{pfx}_x{sl}"])
        c1 = P.emit("vector", lambda h, sl=sl: h.reduce_sum(
            out=cols[:, 0:1], in_=xs[sl][:, :], axis=mybir.AxisListType.X),
            waits=[dsem, ("dve", P.count["dve"])], inc="dve")
        c2 = P.emit("scalar", lambda h, sl=sl: h.activation(
            out=sqd[sl][:, :], in_=xs[sl][:, :], func=AF.Square,
            accum_out=cols[:, 1:2]),
            waits=[dsem, ("dve", P.count["dve"]), ("act", P.count["act"])], inc="act")
        prev["sq"][t] = c2
        c3 = P.emit("vector", lambda h: h.tensor_scalar(
            out=cols[:, 2:3], in0=cols[:, 0:1], scalar1=1.0 / E, scalar2=None,
            op0=OP.mult), waits=[("dve", c1)], inc="dve")
        c4 = P.emit("vector", lambda h: h.tensor_scalar(
            out=cols[:, 3:4], in0=cols[:, 1:2], scalar1=1.0 / E, scalar2=None,
            op0=OP.mult), waits=[("act", c2), ("dve", c3)], inc="dve")
        c5 = P.emit("vector", lambda h: h.tensor_tensor(
            out=cols[:, 4:5], in0=cols[:, 2:3], in1=cols[:, 2:3], op=OP.mult),
            waits=[("dve", c4)], inc="dve")
        c6 = P.emit("vector", lambda h: h.tensor_tensor(
            out=cols[:, 5:6], in0=cols[:, 3:4], in1=cols[:, 4:5], op=OP.subtract),
            waits=[("dve", c5)], inc="dve")
        c7 = P.emit("scalar", lambda h: h.activation(
            out=cols[:, 6:7], in_=cols[:, 5:6], func=AF.Sqrt, bias=eps),
            waits=[("dve", c6)], inc="act")
        c8 = P.emit("vector", lambda h: h.reciprocal(
            out=cols[:, 7:8], in_=cols[:, 6:7]),
            waits=[("act", c7)], inc="dve")
        w2 = [("dve", c8)]
        if t >= 2:
            w2.append(("pe", prev["tp"][t - 2]))
        c9 = P.emit("vector", lambda h, sl=sl: h.tensor_scalar(
            out=xt[sl][:, :], in0=xs[sl][:, :], scalar1=cols[:, 2:3],
            scalar2=cols[:, 7:8], op0=OP.subtract, op1=OP.mult),
            waits=w2, inc="dve")
        prev["aff"][t] = c9
        for j in range(8):
            bv = bva if j % 2 == 0 else bvb
            w3 = [("dve", c9)]
            if t >= 1 or j >= 2:
                key = (t, j - 2) if j >= 2 else (t - 1, 6 + j)
                w3.append(("act", prev["evac"][key]))
            cp = P.emit("tensor", lambda h, sl=sl, j=j, bv=bv: h.transpose(
                out=bv[:, 0:128], in_=xt[sl][:, j * 128:(j + 1) * 128],
                identity=c["eyeb"][:, :]), waits=w3, inc="pe")
            ce = P.emit("scalar", lambda h, j=j, t=t, bv=bv: h.activation(
                out=xn_tiles[j][:, t * 128:(t + 1) * 128], in_=bv[:, 0:128],
                func=AF.Copy), waits=[("pe", cp)], inc="act")
            prev["evac"][(t, j)] = ce
        prev["tp"][t] = P.count["pe"]
    return bag


def emit_tokproj_spill(nc, P, c, xn_tiles, w_tiles, scr, banks, pfx,
                       n_out=E, pair_scratch=True):
    bag = []
    ot = [_alloc(nc, bag, f"{pfx}_o{i}", [128, n_out], BF16) for i in range(2)]
    nh = n_out // 512
    prev_ev = {}
    prev_out = {}
    for t in range(NTT):
        sl = t % 2
        for nhf in range(nh):
            bank = banks[(t * nh + nhf) % 2]
            wmm = []
            if t * nh + nhf >= 2:
                wmm.append(("act", prev_ev[t * nh + nhf - 2]))
            for kd in range(8):
                P.emit("tensor", lambda h, kd=kd, t=t, nhf=nhf, bank=bank:
                       h.matmul(out=bank[:, :],
                                lhsT=xn_tiles[kd][:, t * 128:(t + 1) * 128],
                                rhs=w_tiles[kd][:, nhf * 512:(nhf + 1) * 512],
                                start=(kd == 0), stop=(kd == 7)),
                       waits=wmm if kd == 0 else [], inc="pe" if kd == 7 else None)
            pe_now = P.count["pe"]
            wev = [("pe", pe_now), (f"{pfx}_s{sl}", P.dcount.get(f"{pfx}_s{sl}", 0))]
            ev = P.emit("scalar", lambda h, sl=sl, nhf=nhf, bank=bank: h.activation(
                out=ot[sl][:, nhf * 512:(nhf + 1) * 512], in_=bank[:, :],
                func=AF.Copy), waits=wev, inc="act")
            prev_ev[t * nh + nhf] = ev
        if pair_scratch:
            for p in range(HP):
                P.dma("sync", scr[p, t * 128:(t + 1) * 128, :],
                      ot[sl][:, p * 128:(p + 1) * 128],
                      f"{pfx}_s{sl}", waits=[("act", prev_ev[t * nh + nh - 1])] if p == 0 else [])
        else:
            P.dma("sync", scr[t * 128:(t + 1) * 128, :], ot[sl][:, :],
                  f"{pfx}_s{sl}", waits=[("act", prev_ev[t * nh + nh - 1])])
    return bag



GAMMA_SKIP = 1e-30


def emit_l0_attention(nc, P, c, xn_tiles, wqk, tabs, rcol_d, v_scr, s_scr,
                      ftall, gamma, banks):
    (PA0, PA1, PB0, PB1, PNA, PNB, PSA, PSB) = banks
    PPRJ, PROT = PNA, PNB
    pnv = [PNA[:, :].bitcast(BF16), PNB[:, :].bitcast(BF16)]
    psv = [PSA[:, :].bitcast(BF16), PSB[:, :].bitcast(BF16)]
    wq_tiles, wk_tiles = wqk

    bag = []
    qlam = [_alloc(nc, bag, f"qlam{i}", [128, S], BF16) for i in range(2)]
    klam = [_alloc(nc, bag, f"klam{i}", [128, S], BF16) for i in range(2)]
    vp = [_alloc(nc, bag, f"vp{i}", [128, S], BF16) for i in range(2)]
    tabsl = [[_alloc(nc, bag, f"tab{j}_{i}", [128, 512], BF16)
              for j in range(4)] for i in range(2)]
    qsb = _alloc(nc, bag, "qsb", [128, 512], BF16)
    tt1 = _alloc(nc, bag, "tt1", [128, 512], F32)
    tt2 = _alloc(nc, bag, "tt2", [128, 512], F32)
    asb = [_alloc(nc, bag, f"asb{i}", [128, 512], BF16) for i in range(2)]
    bsb = [_alloc(nc, bag, f"bsb{i}", [128, 512], BF16) for i in range(2)]
    aab = [_alloc(nc, bag, f"aab{i}", [128, 512], BF16) for i in range(2)]
    bab = [_alloc(nc, bag, f"bab{i}", [128, 512], BF16) for i in range(2)]
    sqsb = _alloc(nc, bag, "sqsb", [128, 512], BF16)
    nsb = _alloc(nc, bag, "nsb", [128, 512], BF16)
    stat_sb = _alloc(nc, bag, "stat_sb", [128, 512], BF16)
    dent = _alloc(nc, bag, "dent", [128, 8], F32)
    ssqt = _alloc(nc, bag, "ssqt", [128, 8], F32)
    ep = _alloc(nc, bag, "ep", [128, 64], F32)
    rcsl = [_alloc(nc, bag, f"rcsl{i}", [128, 8], F32) for i in range(2)]
    sout = [_alloc(nc, bag, f"sout{i}", [128, 128], BF16) for i in range(2)]

    done_attn = {}

    def emit_proj(p):
        sl = p % 2
        dep = done_attn.get(p - 2, None)
        wv = [("vp_s0", P.dcount.get("vp_s0", 0)),
              ("vp_s1", P.dcount.get("vp_s1", 0))]
        if dep:
            wv += [("pe", dep["pe"]), ("dve", dep["dve"]), ("act", dep["act"])]
        for i in range(4):
            P.dma("sync", vp[sl][:, i * 1024:(i + 1) * 1024].rearrange(
                      "q (t d) -> q t d", d=128),
                  v_scr[p, i * 1024:(i + 1) * 1024, :].rearrange(
                      "(t q) d -> q t d", q=128),
                  f"vp{sl}", waits=wv if i == 0 else [])
        out = {}
        for tg in range(NTG):
            tsl = tg % 2
            for j, tab in enumerate(tabs):
                wt = [("dve", P.count["dve"])]
                P.dma("sync", tabsl[tsl][j][:, :],
                      tab[p, :, tg * 512:(tg + 1) * 512], f"tb{tsl}_{j}", waits=wt)
            tw = [(f"tb{tsl}_{j}", P.dcount[f"tb{tsl}_{j}"]) for j in range(4)]
            for which, (wt_t, dst, ci, si) in enumerate(
                    [(wq_tiles, qlam[sl], 0, 1), (wk_tiles, klam[sl], 2, 3)]):
                wmm = [("act", P.count["act"])]
                if dep and tg == 0 and which == 0:
                    wmm.append(("dve", dep["dve"]))
                for kd in range(8):
                    P.emit("tensor", lambda h, kd=kd, wt_t=wt_t, tg=tg: h.matmul(
                        out=PPRJ[:, :], lhsT=wt_t[kd][:, p * 128:(p + 1) * 128],
                        rhs=xn_tiles[kd][:, tg * 512:(tg + 1) * 512],
                        start=(kd == 0), stop=(kd == 7)),
                        waits=wmm if kd == 0 else [], inc="pe" if kd == 7 else None)
                pe_p = P.count["pe"]
                e1 = P.emit("scalar", lambda h: h.activation(
                    out=qsb[:, :], in_=PPRJ[:, :], func=AF.Copy),
                    waits=[("pe", pe_p), ("dve", P.count["dve"]),
                           ("act", P.count["act"])], inc="act")
                r1 = P.emit("tensor", lambda h: h.matmul(
                    out=PROT[:, 0:512], lhsT=c["rotT"][:, :], rhs=qsb[:, :],
                    start=True, stop=True),
                    waits=[("act", e1), ("dve", P.count["dve"])], inc="pe")
                wd1 = tw + [("act", e1), ("dve", P.count["dve"])]
                if dep and tg == 0 and which == 0:
                    wd1.append(("pe", dep["pe"]))
                d1 = P.emit("vector", lambda h, tsl=tsl, ci=ci: h.tensor_tensor(
                    out=tt1[:, :], in0=qsb[:, :], in1=tabsl[tsl][ci][:, :],
                    op=OP.mult), waits=wd1, inc="dve")
                d2 = P.emit("vector", lambda h, tsl=tsl, si=si: h.tensor_tensor(
                    out=tt2[:, :], in0=PROT[:, 0:512], in1=tabsl[tsl][si][:, :],
                    op=OP.mult), waits=[("pe", r1), ("dve", d1)], inc="dve")
                d3 = P.emit("vector", lambda h, dst=dst, tg=tg: h.tensor_tensor(
                    out=dst[:, tg * 512:(tg + 1) * 512], in0=tt1[:, :],
                    in1=tt2[:, :], op=OP.mult if False else OP.add),
                    waits=[("dve", d2)], inc="dve")
                out[(tg, which)] = d3
        return out

    def emit_attn(p, proj_done):
        sl = p % 2
        gA = gamma[2 * p]
        gB = gamma[2 * p + 1]
        vwait = (f"vp{sl}", P.dcount[f"vp{sl}"])
        theta_done = max(proj_done.values())
        ev_prev = {}
        for qg in range(NTG):
            kcs = list(range(4 * (qg + 1)))
            skipA = {kc: gA[qg][kc] < GAMMA_SKIP for kc in kcs}
            skipB = {kc: gB[qg][kc] < GAMMA_SKIP for kc in kcs}
            firstA = min(kc for kc in kcs if not skipA[kc])
            firstB = min(kc for kc in kcs if not skipB[kc])
            lastA = max(kc for kc in kcs if not skipA[kc])
            lastB = max(kc for kc in kcs if not skipB[kc])
            rsl = qg % 2
            P.dma("sync", rcsl[rsl][:, :], rcol_d[p, qg, :, :], f"rc{rsl}",
                  waits=[("dve", P.count["dve"])] if (qg >= 2 or p > 0) else [])
            P.emit("vector", lambda h: h.memset(PSA[:, :], 0.0),
                   waits=[("dve", P.count["dve"]), ("pe", P.count["pe"])],
                   inc="dve")
            z0 = P.emit("vector", lambda h: h.memset(PSB[:, :], 0.0),
                        waits=[], inc="dve")
            num_first = {0: True, 1: True}
            den_first = {0: True, 1: True}
            dn_prev = {}

            def emit_front(kc):
                par = kc % 2
                zlo = max(0, kc * 128 - qg * 512)
                res = {}
                for X in range(2):
                    if (skipA[kc] if X == 0 else skipB[kc]):
                        continue
                    bank = (PA0, PA1)[par] if X == 0 else (PB0, PB1)[par]
                    sgn = (asb, bsb)[X][par]
                    ab = (aab, bab)[X][par]
                    gam = gA[qg][kc] if X == 0 else gB[qg][kc]
                    wq = []
                    if (X, par) in ev_prev:
                        wq.append(ev_prev[(X, par)])
                    if kc == firstA and X == 0 or kc == firstB and X == 1:
                        wq.append(("dve", theta_done))
                    q1 = P.emit("tensor", lambda h, X=X, kc=kc, qg=qg, bank=bank:
                                h.matmul(out=bank[:, :],
                                         lhsT=klam[sl][64 * X:64 * X + 64,
                                                       kc * 128:(kc + 1) * 128],
                                         rhs=qlam[sl][64 * X:64 * X + 64,
                                                      qg * 512:(qg + 1) * 512],
                                         start=True, stop=True,
                                         tile_position=(64 * X, 0)),
                                waits=wq, inc="pe")
                    if X == 0:
                        e1 = P.emit("scalar", lambda h, bank=bank, sgn=sgn, gam=gam:
                                    h.activation(out=sgn[:, :], in_=bank[:, :],
                                                 func=AF.Copy, scale=float(gam)),
                                    waits=[("pe", q1), ("act", P.count["act"]),
                                           ("dve", P.count["dve"])], inc="act")
                        eeng = "act"
                    else:
                        e1 = P.emit("vector", lambda h, bank=bank, sgn=sgn, gam=gam:
                                    h.tensor_scalar(out=sgn[:, :], in0=bank[:, :],
                                                    scalar1=float(gam), scalar2=None,
                                                    op0=OP.mult),
                                    waits=[("pe", q1), ("dve", P.count["dve"]),
                                           ("act", P.count["act"])], inc="dve")
                        eeng = "dve"
                    fix = e1
                    if zlo > 0:
                        fix = P.emit("vector", lambda h, sgn=sgn, zlo=zlo: h.memset(
                            sgn[:, 0:zlo], 0.0), waits=[(eeng, e1)], inc="dve")
                    if kc * 128 >= qg * 512:
                        fix = P.emit("vector", lambda h, sgn=sgn, zlo=zlo:
                                     h.tensor_tensor(out=sgn[:, zlo:zlo + 128],
                                                     in0=sgn[:, zlo:zlo + 128],
                                                     in1=c["tril"][:, :], op=OP.mult),
                                     waits=[(eeng, e1),
                                            ("dve", P.count["dve"])], inc="dve")
                    wab = [("pe", dn_prev[(X, par)])] if (X, par) in dn_prev else []
                    if X == 0:
                        a1 = P.emit("scalar", lambda h, sgn=sgn, ab=ab: h.activation(
                            out=ab[:, :], in_=sgn[:, :], func=AF.Abs),
                            waits=wab + [("dve", fix) if fix != e1 else ("act", e1),
                                         ("act", P.count["act"])], inc="act")
                        aeng = "act"
                    else:
                        a1 = P.emit("vector", lambda h, sgn=sgn, ab=ab:
                                    h.tensor_single_scalar(
                                        out=ab[:, :].bitcast(U16),
                                        in_=sgn[:, :].bitcast(U16),
                                        scalar=0x7FFF, op=OP.bitwise_and),
                                    waits=wab + [("dve", P.count["dve"])],
                                    inc="dve")
                        aeng = "dve"
                    ev_prev[(X, par)] = ("act" if X == 0 else "dve", e1)
                    res[X] = (sgn, ab, fix, e1, a1, aeng,
                              "dve" if X == 1 else "act")
                return res

            def emit_back(kc, fr):
                for X, (sgn, ab, fix, e1, a1, aeng, eeng) in fr.items():
                    wn = [("dve", fix) if (fix != e1 or X == 1) else ("act", e1)]
                    if num_first[X]:
                        wn.append(("act", P.count["act"]))
                    lst = (kc == (lastA if X == 0 else lastB))
                    pnx = PNA if X == 0 else PNB
                    n1 = P.emit("tensor", lambda h, X=X, kc=kc, sgn=sgn,
                                st=num_first[X], lst=lst, pnx=pnx:
                                h.matmul(out=pnx[64 * X:64 * X + 64, :],
                                         lhsT=vp[sl][:, kc * 128 + 64 * X:
                                                     kc * 128 + 64 * X + 64],
                                         rhs=sgn[:, :], start=st, stop=lst,
                                         tile_position=(0, 64 * X)),
                                waits=wn + ([vwait] if num_first[X] else []),
                                inc="pe")
                    num_first[X] = False
                    wd = [(aeng, a1)]
                    if den_first[X]:
                        wd.append(("dve", z0))
                    psx = PSA if X == 0 else PSB
                    d1 = P.emit("tensor", lambda h, X=X, ab=ab, st=den_first[X],
                                lst=lst, psx=psx:
                                h.matmul(out=psx[32 * X:32 * X + 1, :],
                                         lhsT=c["ones_col"][:, :], rhs=ab[:, :],
                                         start=st, stop=lst,
                                         tile_position=(0, 32 * X)),
                                waits=wd, inc="pe")
                    den_first[X] = False
                    dn_prev[(X, kc % 2)] = d1

            pending = None
            for kc in kcs:
                fr = emit_front(kc)
                if pending is not None:
                    emit_back(*pending)
                pending = (kc, fr)
            if pending is not None:
                emit_back(*pending)
            pe_attn = P.count["pe"]
            P.emit("scalar", lambda h: h.activation(
                out=sqsb[0:64, :], in_=PNA[0:64, :], func=AF.Square),
                waits=[("pe", pe_attn), ("act", P.count["act"])], inc="act")
            sq1 = P.emit("scalar", lambda h: h.activation(
                out=sqsb[64:128, :], in_=PNB[64:128, :], func=AF.Square),
                waits=[], inc="act")
            s1m = P.emit("tensor", lambda h: h.matmul(
                out=PSA[64:65, :], lhsT=c["onesA_col"][:, :], rhs=sqsb[:, :],
                start=True, stop=True, tile_position=(0, 64)),
                waits=[("act", sq1), ("dve", z0)], inc="pe")
            s2m = P.emit("tensor", lambda h: h.matmul(
                out=PSB[96:97, :], lhsT=c["onesB_col"][:, :], rhs=sqsb[:, :],
                start=True, stop=True, tile_position=(0, 96)),
                waits=[], inc="pe")
            P.emit("scalar", lambda h: h.activation(
                out=nsb[0:64, :], in_=PNA[0:64, :], func=AF.Copy),
                waits=[("act", sq1), ("pe", pe_attn)], inc="act")
            nev = P.emit("scalar", lambda h: h.activation(
                out=nsb[64:128, :], in_=PNB[64:128, :], func=AF.Copy),
                waits=[], inc="act")
            se1 = P.emit("vector", lambda h: h.tensor_copy(
                out=stat_sb[:, :], in_=PSA[:, :]),
                waits=[("pe", s2m), ("dve", P.count["dve"])], inc="dve")
            se2 = P.emit("vector", lambda h: h.tensor_copy(
                out=stat_sb[32:33, :], in_=PSB[32:33, :]),
                waits=[("dve", se1)], inc="dve")
            sev = P.emit("vector", lambda h: h.tensor_copy(
                out=stat_sb[96:97, :], in_=PSB[96:97, :]),
                waits=[("dve", se2)], inc="dve")
            rw = (f"rc{rsl}", P.dcount[f"rc{rsl}"])
            last_ft = None
            for sb in range(4):
                tslot = sb % 2
                st1 = P.emit("tensor", lambda h, sb=sb, tslot=tslot: h.transpose(
                    out=psv[tslot][:, 0:128],
                    in_=stat_sb[:, sb * 128:(sb + 1) * 128],
                    identity=c["eyeb"][:, :]),
                    waits=[("dve", sev if sb < 2 else last_gather[sb - 2])],
                    inc="pe")
                g1 = P.emit("vector", lambda h, sb=sb, tslot=tslot: h.tensor_copy(
                    out=dent[:, sb * 2:sb * 2 + 1],
                    in_=psv[tslot][:, 0:1]),
                    waits=[("pe", st1), ("dve", P.count["dve"])], inc="dve")
                g2 = P.emit("vector", lambda h, sb=sb, tslot=tslot: h.tensor_copy(
                    out=dent[:, sb * 2 + 1:sb * 2 + 2],
                    in_=psv[tslot][:, 32:33]),
                    waits=[("dve", g1)], inc="dve")
                g3 = P.emit("vector", lambda h, sb=sb, tslot=tslot: h.tensor_copy(
                    out=ssqt[:, sb * 2:sb * 2 + 1],
                    in_=psv[tslot][:, 64:65]),
                    waits=[("dve", g2)], inc="dve")
                g4 = P.emit("vector", lambda h, sb=sb, tslot=tslot: h.tensor_copy(
                    out=ssqt[:, sb * 2 + 1:sb * 2 + 2],
                    in_=psv[tslot][:, 96:97]),
                    waits=[("dve", g3)], inc="dve")
                if sb == 0:
                    last_gather = {}
                last_gather[sb] = g4
            m1 = P.emit("vector", lambda h, rsl=rsl: h.tensor_tensor(
                out=ep[:, 0:8], in0=dent[:, :], in1=rcsl[rsl][:, :], op=OP.mult),
                waits=[("dve", last_gather[3]), rw], inc="dve")
            m2 = P.emit("vector", lambda h: h.tensor_scalar(
                out=ep[:, 8:16], in0=ep[:, 0:8], scalar1=50000.0, scalar2=1.0,
                op0=OP.min, op1=OP.max), waits=[("dve", m1)], inc="dve")
            m3 = P.emit("vector", lambda h: h.reciprocal(
                out=ep[:, 16:24], in_=ep[:, 8:16]), waits=[("dve", m2)], inc="dve")
            m4 = P.emit("vector", lambda h, rsl=rsl: h.tensor_tensor(
                out=ep[:, 24:32], in0=ep[:, 16:24], in1=rcsl[rsl][:, :],
                op=OP.mult), waits=[("dve", m3)], inc="dve")
            m5 = P.emit("vector", lambda h: h.tensor_tensor(
                out=ep[:, 32:40], in0=ep[:, 24:32], in1=ep[:, 24:32], op=OP.mult),
                waits=[("dve", m4)], inc="dve")
            m6 = P.emit("vector", lambda h: h.tensor_tensor(
                out=ep[:, 40:48], in0=ep[:, 32:40], in1=ssqt[:, :], op=OP.mult),
                waits=[("dve", m5)], inc="dve")
            m7 = P.emit("vector", lambda h: h.tensor_scalar(
                out=ep[:, 48:56], in0=ep[:, 40:48], scalar1=1.0 / 64, scalar2=1e-6,
                op0=OP.mult, op1=OP.add), waits=[("dve", m6)], inc="dve")
            m8 = P.emit("scalar", lambda h: h.activation(
                out=ep[:, 56:64], in_=ep[:, 48:56], func=AF.Sqrt),
                waits=[("dve", m7)], inc="act")
            m9 = P.emit("vector", lambda h: h.reciprocal(
                out=ep[:, 40:48], in_=ep[:, 56:64]), waits=[("act", m8)], inc="dve")
            m10 = P.emit("vector", lambda h: h.tensor_tensor(
                out=ep[:, 48:56], in0=ep[:, 24:32], in1=ep[:, 40:48], op=OP.mult),
                waits=[("dve", m9)], inc="dve")
            for sb in range(4):
                tt = qg * 4 + sb
                P.emit("vector", lambda h, sb=sb, tt=tt: h.tensor_copy(
                    out=ftall[:, tt * 16 + 2 * p:tt * 16 + 2 * p + 2],
                    in_=ep[:, 48 + sb * 2:48 + sb * 2 + 2]),
                    waits=[("dve", m10)], inc="dve")
            for sb in range(4):
                tslot = sb % 2
                w1t = [("act", nev)]
                if sb >= 2:
                    w1t.append(("act", last_sev[sb - 2]))
                t1 = P.emit("tensor", lambda h, sb=sb, tslot=tslot: h.transpose(
                    out=pnv[tslot][:, 0:128],
                    in_=nsb[:, sb * 128:(sb + 1) * 128],
                    identity=c["eyeb"][:, :]), waits=w1t, inc="pe")
                ssl = sb % 2
                e2 = P.emit("scalar", lambda h, tslot=tslot, ssl=ssl: h.activation(
                    out=sout[ssl][:, :], in_=pnv[tslot][:, 0:128],
                    func=AF.Copy),
                    waits=[("pe", t1),
                           (f"ss{ssl}", P.dcount.get(f"ss{ssl}", 0))], inc="act")
                P.dma("sync", s_scr[p, (qg * 4 + sb) * 128:(qg * 4 + sb + 1) * 128, :],
                      sout[ssl][:, :], f"ss{ssl}", waits=[("act", e2)])
                if sb == 0:
                    last_sev = {}
                last_sev[sb] = e2
        return {"pe": P.count["pe"], "dve": P.count["dve"], "act": P.count["act"]}

    for p in range(HP):
        proj_done = emit_proj(p)
        done_attn[p] = emit_attn(p, proj_done)
    return bag


def emit_gating(nc, P, c, xn_tiles, gw_tiles, selw_all, selwT_sb, banks, pfx,
                bag):
    PL, PT = banks
    ptv = PT
    g8 = _alloc(nc, bag, f"{pfx}_g8", [128, 64], F32)
    gc = _alloc(nc, bag, f"{pfx}_gc", [128, 8], F32)
    prev_exp = 0
    prev_tr = 0
    for t in range(NTT):
        wmm = [("act", prev_exp), ("dve", P.count["dve"])] if t > 0 else []
        for kd in range(8):
            P.emit("tensor", lambda h, kd=kd, t=t: h.matmul(
                out=PL[:, 0:8], lhsT=xn_tiles[kd][:, t * 128:(t + 1) * 128],
                rhs=gw_tiles[kd][:, :], start=(kd == 0), stop=(kd == 7)),
                waits=wmm if kd == 0 else [], inc="pe" if kd == 7 else None)
        pe_n = P.count["pe"]
        v1 = P.emit("vector", lambda h: h.reduce_max(
            out=gc[:, 0:1], in_=PL[:, 0:8], axis=mybir.AxisListType.X),
            waits=[("pe", pe_n), ("dve", P.count["dve"])], inc="dve")
        v2 = P.emit("vector", lambda h: h.tensor_scalar(
            out=gc[:, 1:2], in0=gc[:, 0:1], scalar1=-1.0, scalar2=None,
            op0=OP.mult), waits=[("dve", v1)], inc="dve")
        a1 = P.emit("scalar", lambda h: h.activation(
            out=g8[:, 0:8], in_=PL[:, 0:8], func=AF.Exp, bias=gc[:, 1:2]),
            waits=[("dve", v2), ("act", prev_exp)], inc="act")
        prev_exp = a1
        v3 = P.emit("vector", lambda h: h.reduce_sum(
            out=gc[:, 2:3], in_=g8[:, 0:8], axis=mybir.AxisListType.X),
            waits=[("act", a1), ("dve", P.count["dve"])], inc="dve")
        v4 = P.emit("vector", lambda h: h.reciprocal(
            out=gc[:, 3:4], in_=gc[:, 2:3]), waits=[("dve", v3)], inc="dve")
        v5 = P.emit("vector", lambda h: h.tensor_scalar(
            out=g8[:, 8:16], in0=g8[:, 0:8], scalar1=gc[:, 3:4], scalar2=None,
            op0=OP.mult), waits=[("dve", v4)], inc="dve")
        v6 = P.emit("vector", lambda h: h.reduce_max(
            out=gc[:, 4:5], in_=g8[:, 8:16], axis=mybir.AxisListType.X),
            waits=[("dve", v5)], inc="dve")
        v7 = P.emit("vector", lambda h: h.tensor_scalar(
            out=g8[:, 16:24], in0=g8[:, 8:16], scalar1=gc[:, 4:5], scalar2=None,
            op0=OP.is_equal), waits=[("dve", v6)], inc="dve")
        v8 = P.emit("vector", lambda h: h.scalar_tensor_tensor(
            out=g8[:, 24:32], in0=g8[:, 16:24], scalar=-1e30, in1=g8[:, 8:16],
            op0=OP.mult, op1=OP.add), waits=[("dve", v7)], inc="dve")
        v9 = P.emit("vector", lambda h: h.reduce_max(
            out=gc[:, 5:6], in_=g8[:, 24:32], axis=mybir.AxisListType.X),
            waits=[("dve", v8)], inc="dve")
        v10 = P.emit("vector", lambda h: h.tensor_tensor(
            out=gc[:, 6:7], in0=gc[:, 4:5], in1=gc[:, 5:6], op=OP.add),
            waits=[("dve", v9)], inc="dve")
        v11 = P.emit("vector", lambda h, t=t: h.tensor_scalar(
            out=selw_all[:, t * 8:(t + 1) * 8], in0=g8[:, 8:16],
            scalar1=gc[:, 5:6], scalar2=gc[:, 6:7], op0=OP.is_ge, op1=OP.mult),
            waits=[("dve", v10)], inc="dve")
        if selwT_sb is not None:
            tr1 = P.emit("tensor", lambda h, t=t: h.transpose(
                out=ptv[0:8, 0:128], in_=selw_all[:, t * 8:(t + 1) * 8],
                identity=c["eyef"][:, :]),
                waits=[("dve", v11)] + ([("dve", prev_tr)] if t else []), inc="pe")
            prev_tr = P.emit("vector", lambda h, t=t: h.tensor_copy(
                out=selwT_sb[0:8, t * 128:(t + 1) * 128], in_=ptv[0:8, 0:128]),
                waits=[("pe", tr1)], inc="dve")


def emit_moe0_combine(nc, P, c, xn_tiles, x_dram, x1_dst, w1b_scr, w2b_scr,
                      b1T, b2b, selw_all, selwT_sb, s_scr, g_scr, ftall, banks):
    (PH0b, PH1b, PY0, PY1, PB2) = banks
    bag = []
    w1buf = [_alloc(nc, bag, f"m0w1_{i}", [128, 4096], BF16) for i in range(2)]
    w2buf = [_alloc(nc, bag, f"m0w2_{i}", [128, 4096], BF16) for i in range(2)]
    he = [_alloc(nc, bag, f"m0h_{i}", [128, 2048], BF16) for i in range(2)]
    yacc = [_alloc(nc, bag, f"m0y_{i}", [128, 1024], F32) for i in range(4)]
    gm = _alloc(nc, bag, "m0gm", [128, 1024], BF16)
    sin_ = [_alloc(nc, bag, f"m0s_{i}", [128, 1024], BF16) for i in range(2)]
    gin = [_alloc(nc, bag, f"m0g_{i}", [128, 1024], BF16) for i in range(2)]
    xf = [_alloc(nc, bag, f"m0x_{i}", [128, 1024], F32) for i in range(2)]
    gsil = _alloc(nc, bag, "m0gsil", [128, 1024], BF16)

    last_h_read = {}
    last_w1 = {}
    last_w2 = {}
    last_comb = {}
    for tg in range(NTG):
        ybank = {}
        for tt in range(4):
            tglob = tg * 4 + tt
            for half in range(2):
                bidx = (tt * 2 + half) % 2
                bank = (PY0, PY1)[bidx]
                wb2 = [("dve", P.count["dve"])]
                b1m = P.emit("tensor", lambda h, tglob=tglob, half=half, bank=bank:
                             h.matmul(out=bank[:, :],
                                      lhsT=selwT_sb[0:8, tglob * 128:(tglob + 1) * 128],
                                      rhs=b2b[0:8, half * 512:(half + 1) * 512],
                                      start=True, stop=True),
                             waits=wb2, inc="pe")
                ybank[bidx] = P.emit(
                    "vector", lambda h, tt=tt, half=half, bank=bank: h.tensor_copy(
                        out=yacc[tt][:, half * 512:(half + 1) * 512], in_=bank[:, :]),
                    waits=[("pe", b1m), ("dve", P.count["dve"])], inc="dve")
        for e in range(NEXP):
            sl = e % 2
            ww = [("act", last_w1.get(sl, 0)), ("gps", P.count["gps"]), ("gco0", P.dcount.get("gco0", 0)), ("gco1", P.dcount.get("gco1", 0))]
            P.dma("sync", w1buf[sl][:, :].rearrange("p (kd c) -> p kd c", c=512),
                  w1b_scr[e].rearrange("(kd p) c -> p kd c", p=128),
                  f"m0w1_{sl}", waits=ww)
            ww2 = [("pe", last_w2.get(sl, 0)), ("gps", P.count["gps"]), ("gco0", P.dcount.get("gco0", 0)), ("gco1", P.dcount.get("gco1", 0))]
            P.dma("sync", w2buf[sl][:, :].rearrange("p (kd c) -> p kd c", c=1024),
                  w2b_scr[e].rearrange("(kd p) c -> p kd c", p=128),
                  f"m0w2_{sl}", waits=ww2)
            w1w = (f"m0w1_{sl}", P.dcount[f"m0w1_{sl}"])
            w2w = (f"m0w2_{sl}", P.dcount[f"m0w2_{sl}"])
            hevs = {}
            hev = 0
            for hct in range(4):
                bank = (PH0b, PH1b)[hct % 2]
                wmm = [("act", hevs[hct - 2])] if hct >= 2 else []
                if hct == 0:
                    wmm = [w1w, ("pe", last_h_read.get(sl, 0)),
                           ("act", P.count["act"])]
                for kd in range(8):
                    P.emit("tensor", lambda h, kd=kd, hct=hct, tg=tg, bank=bank, sl=sl:
                           h.matmul(out=bank[:, :],
                                    lhsT=w1buf[sl][:, kd * 512 + hct * 128:
                                                   kd * 512 + (hct + 1) * 128],
                                    rhs=xn_tiles[kd][:, tg * 512:(tg + 1) * 512],
                                    start=(kd == 0), stop=(kd == 7)),
                           waits=wmm if kd == 0 else [], inc="pe" if kd == 7 else None)
                pe_h = P.count["pe"]
                hev = P.emit("scalar", lambda h, hct=hct, sl=sl, e=e, bank=bank:
                             h.activation(out=he[sl][:, hct * 512:(hct + 1) * 512],
                                          in_=bank[:, :], func=AF.Gelu,
                                          bias=b1T[:, e * 4 + hct:e * 4 + hct + 1]),
                             waits=[("pe", pe_h), ("act", P.count["act"])], inc="act")
                hevs[hct] = hev
            last_w1[sl] = hev
            for tt in range(4):
                tglob = tg * 4 + tt
                for half in range(2):
                    bidx = (tt * 2 + half) % 2
                    bank = (PY0, PY1)[bidx]
                    wmm = [("act", hev), w2w, ("dve", ybank[bidx])]
                    for hc in range(4):
                        P.emit("tensor", lambda h, hc=hc, tt=tt, half=half, sl=sl, bank=bank:
                               h.matmul(out=bank[:, :],
                                        lhsT=he[sl][:, hc * 512 + tt * 128:
                                                    hc * 512 + (tt + 1) * 128],
                                        rhs=w2buf[sl][:, hc * 1024 + half * 512:
                                                      hc * 1024 + (half + 1) * 512],
                                        start=(hc == 0), stop=(hc == 3)),
                               waits=wmm if hc == 0 else [],
                               inc="pe" if hc == 3 else None)
                    pe_y = P.count["pe"]
                    ybank[bidx] = P.emit(
                        "vector", lambda h, tt=tt, half=half, e=e, tglob=tglob, bank=bank:
                        h.scalar_tensor_tensor(
                            out=yacc[tt][:, half * 512:(half + 1) * 512],
                            in0=bank[:, :],
                            scalar=selw_all[:, tglob * 8 + e:tglob * 8 + e + 1],
                            in1=yacc[tt][:, half * 512:(half + 1) * 512],
                            op0=OP.mult, op1=OP.add),
                        waits=[("pe", pe_y), ("dve", P.count["dve"])], inc="dve")
            last_h_read[sl] = P.count["pe"]
            last_w2[sl] = P.count["pe"]
        for tt in range(4):
            tglob = tg * 4 + tt
            sl = tglob % 2
            wld = [("dve", last_comb.get(sl, 0)),
                   (f"x1o{sl}", P.dcount.get(f"x1o{sl}", 0))]
            P.dma("sync", xf[sl][:, :],
                  x_dram[tglob * 128:(tglob + 1) * 128, :], f"cx{sl}", waits=wld)
            for p in range(HP):
                wss = (wld + [("ss0", P.dcount.get("ss0", 0)),
                              ("ss1", P.dcount.get("ss1", 0))]) if p == 0 else []
                P.dma("sync", sin_[sl][:, p * 128:(p + 1) * 128],
                      s_scr[p, tglob * 128:(tglob + 1) * 128, :], f"cs{sl}",
                      waits=wss)
                wgg = [("gp_s0", P.dcount.get("gp_s0", 0)),
                       ("gp_s1", P.dcount.get("gp_s1", 0))] if p == 0 else []
                P.dma("sync", gin[sl][:, p * 128:(p + 1) * 128],
                      g_scr[p, tglob * 128:(tglob + 1) * 128, :], f"cg{sl}",
                      waits=wgg)
            a1 = P.emit("scalar", lambda h, sl=sl: h.activation(
                out=gsil[:, :], in_=gin[sl][:, :], func=AF.Silu),
                waits=[(f"cg{sl}", P.dcount[f"cg{sl}"]), ("act", P.count["act"]),
                       ("dve", P.count["dve"])], inc="act")
            sw = [(f"cs{sl}", P.dcount[f"cs{sl}"]), ("dve", P.count["dve"])]
            for hh in range(16):
                P.emit("vector", lambda h, hh=hh, sl=sl, tglob=tglob: h.tensor_scalar(
                    out=sin_[sl][:, hh * 64:(hh + 1) * 64],
                    in0=sin_[sl][:, hh * 64:(hh + 1) * 64],
                    scalar1=ftall[:, tglob * 16 + hh:tglob * 16 + hh + 1],
                    scalar2=None, op0=OP.mult),
                    waits=sw if hh == 0 else [], inc="dve")
            c1 = P.emit("vector", lambda h, sl=sl: h.tensor_tensor(
                out=sin_[sl][:, :], in0=sin_[sl][:, :], in1=gsil[:, :], op=OP.mult),
                waits=[("act", a1), ("dve", P.count["dve"])], inc="dve")
            a2 = P.emit("scalar", lambda h, tt=tt: h.activation(
                out=gm[:, :], in_=yacc[tt][:, :], func=AF.Silu),
                waits=[("dve", P.count["dve"]), ("act", P.count["act"])], inc="act")
            c3 = P.emit("vector", lambda h, sl=sl: h.tensor_tensor(
                out=sin_[sl][:, :], in0=sin_[sl][:, :], in1=gm[:, :], op=OP.mult),
                waits=[("act", a2), ("dve", P.count["dve"])], inc="dve")
            c4 = P.emit("vector", lambda h, sl=sl: h.tensor_tensor(
                out=xf[sl][:, :], in0=xf[sl][:, :], in1=sin_[sl][:, :], op=OP.add),
                waits=[("dve", c3), (f"cx{sl}", P.dcount[f"cx{sl}"])], inc="dve")
            P.dma("sync", x1_dst[tglob * 128:(tglob + 1) * 128, :],
                  xf[sl][:, :], f"x1o{sl}", waits=[("dve", c4)])
            last_comb[sl] = c4
    return bag



def make_gps_casts(nc, P, jobs, pfx="gc"):
    cbag = []

    def _lalloc(name, shape, dt):
        cm = nc.sbuf_tensor(name, shape, dt)
        h = cm.__enter__()
        cbag.append(cm)
        return h
    stage = [_lalloc(f"{pfx}_st{i}", [128, 1024], F32) for i in range(2)]
    obuf = [_lalloc(f"{pfx}_ob{i}", [128, 1024], BF16) for i in range(2)]
    steps = []
    state = {"i": 0, "last": {}, "extra": [], "extra_uses": 0}

    def one(src, dst, r, c0, cols):
        def go(src=src, dst=dst, r=r, c0=c0, cols=cols):
            i = state["i"]
            state["i"] += 1
            sl = i % 2
            wv = [("gps", state["last"].get(sl, 0))]
            if state["extra_uses"] > 0:
                wv = wv + state["extra"]
                state["extra_uses"] -= 1
            P.dma("gpsimd", stage[sl][:, 0:cols],
                  src[r * 128:(r + 1) * 128, c0:c0 + cols],
                  f"{pfx}i{sl}", waits=wv)
            cc = P.emit("gpsimd", lambda h, sl=sl, cols=cols: h.tensor_copy(
                out=obuf[sl][:, 0:cols], in_=stage[sl][:, 0:cols]),
                waits=[(f"{pfx}i{sl}", P.dcount[f"{pfx}i{sl}"]),
                       (f"{pfx}o{sl}", P.dcount.get(f"{pfx}o{sl}", 0))], inc="gps")
            P.dma("gpsimd", dst[r * 128:(r + 1) * 128, c0:c0 + cols],
                  obuf[sl][:, 0:cols], f"{pfx}o{sl}", waits=[("gps", cc)])
            state["last"][sl] = cc
        return go

    for src, dst in jobs:
        rows, cols = src.shape
        assert rows % 128 == 0
        for r in range(rows // 128):
            for c0 in range(0, cols, 1024):
                steps.append(one(src, dst, r, c0, min(1024, cols - c0)))
    return steps, cbag, state


def cast_weight_tiles(nc, P, src_dram, tiles, eng, pfx, stage):
    esem = {"vector": "dve", "scalar": "act"}[eng]
    n = len(tiles)
    last = {}
    for i in range(n):
        sl = i % 2
        P.dma("sync", stage[sl][:, 0:tiles[i].shape[1]],
              src_dram[i * 128:(i + 1) * 128, :], f"{pfx}{sl}",
              waits=[(esem, last.get(sl, 0))])
        if eng == "scalar":
            cc = P.emit(eng, lambda h, i=i, sl=sl: h.copy(
                out=tiles[i][:, :], in_=stage[sl][:, 0:tiles[i].shape[1]]),
                waits=[(f"{pfx}{sl}", P.dcount[f"{pfx}{sl}"]),
                       (esem, P.count[esem])], inc=esem)
        else:
            cc = P.emit(eng, lambda h, i=i, sl=sl: h.tensor_copy(
                out=tiles[i][:, :], in_=stage[sl][:, 0:tiles[i].shape[1]]),
                waits=[(f"{pfx}{sl}", P.dcount[f"{pfx}{sl}"]),
                       (esem, P.count[esem])], inc=esem)
        last[sl] = cc
    return P.count[esem]


def build_layer0(nc, dbg=None, P=None, banks=None, x1_dst=None, moe_scr=None):
    own = P is None
    if own:
        P = Plan(nc)
        for s in ["pe", "dve", "act", "gps"]:
            P.sem(s)
        c = _consts(nc, P)
        ConstDB(nc, P, _l0_const_vals(), [-1.0, 1.0])
    else:
        c = P._c
    gamma = TAB["gamma"]

    x_d = nc.dram_tensor("xin", [S, E], F32, kind="ExternalInput").ap()
    if x1_dst is None:
        x1_d = nc.dram_tensor("x1out", [S, E], F32, kind="ExternalOutput").ap()
    else:
        x1_d = x1_dst
    Wq_d = nc.dram_tensor("Wq", [E, E], F32, kind="ExternalInput").ap()
    Wk_d = nc.dram_tensor("Wk", [E, E], F32, kind="ExternalInput").ap()
    Wv_d = nc.dram_tensor("Wv", [E, E], F32, kind="ExternalInput").ap()
    Wg_d = nc.dram_tensor("Wg", [E, E], F32, kind="ExternalInput").ap()
    gw_d = nc.dram_tensor("gate_w", [E, 8], F32, kind="ExternalInput").ap()
    b1T_d = nc.dram_tensor("b1T", [128, 32], F32, kind="ExternalInput").ap()
    b2_d = nc.dram_tensor("b2", [NEXP, E], F32, kind="ExternalInput").ap()
    if moe_scr is None:
        w1_d = nc.dram_tensor("m_w1", [NEXP, E, 512], F32, kind="ExternalInput").ap()
        w2_d = nc.dram_tensor("m_w2", [NEXP, 512, E], F32, kind="ExternalInput").ap()
    tabs = []
    for nm in ["cosq", "sinq", "cosk", "sink"]:
        tabs.append(nc.dram_tensor(nm, [HP, 128, S], BF16, kind="ExternalInput").ap())
    rcol_d = nc.dram_tensor("rcol", [HP, NTG, 128, 8], F32, kind="ExternalInput").ap()

    kv = "ExternalOutput" if dbg in ("ph0v",) else "Internal"
    ks = "ExternalOutput" if dbg in ("ph1",) else "Internal"
    v_scr = nc.dram_tensor("v_scr", [HP, S, 128], BF16, kind=kv).ap()
    g_scr = nc.dram_tensor("g_scr", [HP, S, 128], BF16, kind=kv).ap()
    s_scr = nc.dram_tensor("s_scr", [HP, S, 128], BF16, kind=ks).ap()
    if moe_scr is None:
        w1b_scr = nc.dram_tensor("w1b_scr", [NEXP, E, 512], BF16, kind="Internal").ap()
        w2b_scr = nc.dram_tensor("w2b_scr", [NEXP, 512, E], BF16, kind="Internal").ap()
    else:
        w1b_scr, w2b_scr = moe_scr

    if banks is None:
        banks = [nc.psum_tensor(f"pb{i}", [128, 512], F32).__enter__() for i in range(8)]

    l0bag = []
    xn0 = [_alloc(nc, l0bag, f"xn0_{i}", [128, S], BF16) for i in range(8)]
    WA = [_alloc(nc, l0bag, f"WA{i}", [128, E], BF16) for i in range(8)]
    WB = [_alloc(nc, l0bag, f"WB{i}", [128, E], BF16) for i in range(8)]
    gwb = [_alloc(nc, l0bag, f"gwb{i}", [128, 8], BF16) for i in range(8)]
    b1T = _alloc(nc, l0bag, "b1Ts", [128, 32], F32)
    b2b = _alloc(nc, l0bag, "b2b", [8, E], BF16)
    b2s = _alloc(nc, l0bag, "b2s", [8, E], F32)
    ftall = _alloc(nc, l0bag, "ftall", [128, NTT * 16], F32)
    selw_all = _alloc(nc, l0bag, "selw0", [128, NTT * 8], F32)
    selwT_sb = _alloc(nc, l0bag, "selwT0", [8, S], BF16)

    P.dma("sync", b1T[:, :], b1T_d, "small")
    P.dma("sync", b2s[:, :], b2_d, "small")
    P.barrier(extra_dma=("const", "small"))
    P.init_psum(banks, c["eyeb"], "const")
    P.emit("vector", lambda h: h.tensor_copy(out=b2b[:, :], in_=b2s[:, :]),
           waits=[("small", P.dcount["small"])], inc="dve")

    if moe_scr is None:
        jobs = []
        for e in range(NEXP):
            jobs.append((w1_d[e], w1b_scr[e]))
        for e in range(NEXP):
            jobs.append((w2_d[e], w2b_scr[e]))
        casts, castbag, _cst = make_gps_casts(nc, P, jobs)
    else:
        casts, castbag = [], []
    stgbag = []
    stage = [_alloc(nc, stgbag, f"wst{i}", [128, E], F32) for i in range(2)]
    stage2 = [_alloc(nc, stgbag, f"ws2_{i}", [128, E], F32) for i in range(2)]
    cast_i = [0]

    def drain(n):
        k = 0
        while cast_i[0] < len(casts) and k < n:
            casts[cast_i[0]]()
            cast_i[0] += 1
            k += 1

    cast_weight_tiles(nc, P, Wv_d, WA, "vector", "cwv", stage)
    cast_weight_tiles(nc, P, Wg_d, WB, "scalar", "cwg", stage2)
    for i in range(8):
        P.dma("sync", stage[0][:, 0:8], gw_d[i * 128:(i + 1) * 128, :], "cgw",
              waits=[("dve", P.count["dve"])])
        P.emit("vector", lambda h, i=i: h.tensor_copy(
            out=gwb[i][:, :], in_=stage[0][:, 0:8]),
            waits=[("cgw", P.dcount["cgw"])], inc="dve")
    drain(8)

    lnbag = emit_ln_feature(nc, P, c, x_d, xn0, (banks[0], banks[1]), "ln0")
    P.barrier()
    _free(lnbag)
    drain(8)

    b1_ = emit_tokproj_spill(nc, P, c, xn0, WA, v_scr, (banks[0], banks[1]), "vp")
    b2_ = emit_tokproj_spill(nc, P, c, xn0, WB, g_scr, (banks[2], banks[3]), "gp")
    P.barrier(extra_dma=("vp_s0", "vp_s1", "gp_s0", "gp_s1"))
    _free(b2_); _free(b1_)
    drain(8)

    cast_weight_tiles(nc, P, Wq_d, WA, "vector", "cwq", stage)
    cast_weight_tiles(nc, P, Wk_d, WB, "scalar", "cwk", stage2)
    P.barrier()

    if dbg == "ph0v":
        P.emit("gpsimd", lambda h: h.nop(),
               waits=[("vp_s0", P.dcount["vp_s0"]), ("vp_s1", P.dcount["vp_s1"]),
                      ("gp_s0", P.dcount["gp_s0"]), ("gp_s1", P.dcount["gp_s1"])])
        P.finalize()
        return

    if dbg == "ph0":
        dbg_d = nc.dram_tensor("dbg", [8, 128, S], BF16, kind="ExternalOutput").ap()
        for i in range(8):
            P.dma("gpsimd", dbg_d[i], xn0[i][:, :], "dbgo", waits=[])
        P.emit("gpsimd", lambda h: h.nop(), waits=[("dbgo", P.dcount["dbgo"])])
        P.finalize()
        return

    _free(stgbag)
    drain(30)
    atbag = emit_l0_attention(nc, P, c, xn0, (WA, WB), tabs, rcol_d, v_scr, s_scr,
                              ftall, gamma, banks)
    P.barrier(extra_dma=("ss0", "ss1"))
    _free(atbag)
    drain(len(casts))
    _free(castbag)

    if dbg == "ph1":
        dbg_d = nc.dram_tensor("dbg", [128, NTT * 16], BF16, kind="ExternalOutput").ap()
        P.dma("gpsimd", dbg_d, ftall[:, :], "dbgo", waits=[])
        P.emit("gpsimd", lambda h: h.nop(), waits=[("dbgo", P.dcount["dbgo"])])
        P.finalize()
        return

    emit_gating(nc, P, c, xn0, gwb, selw_all, selwT_sb, (banks[4], banks[5]),
                "g0", l0bag)
    P.barrier(extra_dma=("gco0", "gco1", "gci0", "gci1"))
    m0bag = emit_moe0_combine(nc, P, c, xn0, x_d, x1_d, w1b_scr, w2b_scr, b1T,
                              b2b, selw_all, selwT_sb, s_scr, g_scr, ftall,
                              (banks[0], banks[1], banks[2], banks[3], banks[6]))
    P.barrier()
    if m0bag:
        _free(m0bag)
    _free(l0bag)
    if own:
        P.finalize()



def build_fused(nc):
    P = Plan(nc)
    for s in ["pe", "dve", "act", "gps"]:
        P.sem(s)
    c = _consts(nc, P)
    P._c = c
    ConstDB(nc, P, _l0_const_vals() + [1e-5], [-1.0, 1.0])
    banks = [nc.psum_tensor(f"pb{i}", [128, 512], F32).__enter__()
             for i in range(8)]
    x1_scr = nc.dram_tensor("x1mid", [S, E], F32, kind="Internal").ap()
    selw1 = nc.sbuf_tensor("selw1", [128, NTT * 8], F32).__enter__()
    gwb1 = [nc.sbuf_tensor(f"gwb1_{i}", [128, 8], BF16).__enter__()
            for i in range(8)]
    w1_d = nc.dram_tensor("m_w1", [NEXP, E, 512], F32, kind="ExternalInput").ap()
    w2_d = nc.dram_tensor("m_w2", [NEXP, 512, E], F32, kind="ExternalInput").ap()
    w1b_scr = nc.dram_tensor("w1b_scr", [NEXP, E, 512], BF16, kind="Internal").ap()
    w2b_scr = nc.dram_tensor("w2b_scr", [NEXP, 512, E], BF16, kind="Internal").ap()
    mw1_d = nc.dram_tensor("m1_w1T", [NEXP, E, 4 * E], F32, kind="ExternalInput").ap()
    mw2_d = nc.dram_tensor("m1_w2T", [NEXP, 4 * E, E], F32, kind="ExternalInput").ap()
    w1b1_scr = nc.dram_tensor("w1b1_scr", [NEXP, E, 4 * E], BF16, kind="Internal").ap()
    w2b1_scr = nc.dram_tensor("w2b1_scr", [NEXP, 4 * E, E], BF16, kind="Internal").ap()
    jobs = []
    for e in range(NEXP):
        jobs.append((w1_d[e], w1b_scr[e]))
    for e in range(NEXP):
        jobs.append((w2_d[e], w2b_scr[e]))
    nl0 = NEXP * 8 + NEXP * 4
    for e in range(NEXP):
        jobs.append((mw1_d[e], w1b1_scr[e]))
    for e in range(NEXP):
        jobs.append((mw2_d[e], w2b1_scr[e]))
    casts, castbag, cstate = make_gps_casts(nc, P, jobs)
    for go in casts[:nl0]:
        go()
    P.barrier(extra_dma=("const",))
    build_layer0(nc, P=P, banks=banks, x1_dst=x1_scr,
                 moe_scr=(w1b_scr, w2b_scr))
    P.barrier()
    build_layer1(nc, x1_src=x1_scr, P=P, banks=banks,
                 moe_scr=(w1b1_scr, w2b1_scr),
                 caster=(casts[nl0:], cstate, castbag),
                 router_sb=(selw1, gwb1))
    P.finalize()


def _layer0_inputs(inputs, b, tab):
    return {
        "xin": np.ascontiguousarray(inputs["L0_attn_x"][b]),
        "Wq": inputs["L0_attn_Wq"], "Wk": inputs["L0_attn_Wk"],
        "Wv": inputs["L0_attn_Wv"], "Wg": inputs["L0_attn_Wg"],
        "gate_w": inputs["L0_ffn_gate_w"],
        "m_w1": inputs["L0_ffn_w1"],
        "b1T": np.ascontiguousarray(
            inputs["L0_ffn_b1"].reshape(NEXP * 4, 128).T),
        "m_w2": inputs["L0_ffn_w2"],
        "b2": inputs["L0_ffn_b2"],
        "cosq": tab["cosq"], "sinq": tab["sinq"],
        "cosk": tab["cosk"], "sink": tab["sink"],
        "rcol": tab["rcol"],
        "eyeb": tab["eyeb"], "rotT": tab["rotT"], "tril": tab["tril"],
        "ones_col": tab["ones_col"], "onesA_col": tab["onesA_col"],
        "onesB_col": tab["onesB_col"], "ones_row": tab["ones_row"],
        "eyef": tab["eyef"], "mixrow": tab["mixrow"], "mixrowf": tab["mixrowf"], "bc128": tab["bc128"],
        "onesA_row": tab["onesA_row"], "onesB_row": tab["onesB_row"],
        "cdbf32": _cdb_host_arrays(_l0_const_vals() + [1e-5], [-1.0, 1.0])[0],
        "cdbbf16": _cdb_host_arrays(_l0_const_vals() + [1e-5], [-1.0, 1.0])[1],
    }


def _fused_inputs(inputs, b, tab):
    m = _layer0_inputs(inputs, b, tab)
    m1 = _layer1_inputs(inputs, b, tab, np.zeros((1, 1), np.float32))
    del m1["x1in"]
    m.update(m1)
    m["cdbf32"], m["cdbbf16"] = _cdb_host_arrays(
        _l0_const_vals() + [1e-5], [-1.0, 1.0])
    return m


def kernel(**inputs):
    global TAB
    TAB = host_tables()
    nc = bass.Bass("TRN2", target_bir_lowering=False)
    build_fused(nc)
    in_maps = [_fused_inputs(inputs, b, TAB) for b in range(8)]
    res = run_bass_kernel_spmd(nc, in_maps, core_ids=list(range(8)))
    return np.stack([res.results[b]["out"] for b in range(8)])



def emit_ln2_feature(nc, P, c, x_dram, xnn_tiles, xn1_scr, gwb1, selw1_all,
                     banks, pfx):
    bag = []
    xs = [_alloc(nc, bag, f"{pfx}_xs{i}", [128, E], F32) for i in range(2)]
    xa = [_alloc(nc, bag, f"{pfx}_xa{i}", [128, E], F32) for i in range(2)]
    xt = [_alloc(nc, bag, f"{pfx}_xt{i}", [128, E], BF16) for i in range(2)]
    xt2 = [_alloc(nc, bag, f"{pfx}_xu{i}", [128, E], BF16) for i in range(2)]
    sqd = [_alloc(nc, bag, f"{pfx}_sq{i}", [128, E], F32) for i in range(2)]
    cols = _alloc(nc, bag, f"{pfx}_cols", [128, 16], F32)
    x1tmp = [_alloc(nc, bag, f"{pfx}_x1t{i}", [128, E], BF16) for i in range(2)]
    g8 = _alloc(nc, bag, f"{pfx}_g8", [128, 32], F32)
    gc = _alloc(nc, bag, f"{pfx}_gc", [128, 8], F32)
    bk_a, bk_b, bk_l = banks
    bva = bk_a[:, :].bitcast(BF16)
    bvb = bk_b[:, :].bitcast(BF16)
    prev = {"aff": {}, "tp": {}, "evac": {}, "sq": {}, "exp": 0, "gl": {}}

    def stats(src_sl, c_mu, c_r, dsem, eps):
        c1 = P.emit("vector", lambda h, src_sl=src_sl: h.reduce_sum(
            out=cols[:, 0:1], in_=src_sl[:, :], axis=mybir.AxisListType.X),
            waits=dsem + [("dve", P.count["dve"])], inc="dve")
        c2 = P.emit("scalar", lambda h, src_sl=src_sl: h.activation(
            out=sqd[0][:, :], in_=src_sl[:, :], func=AF.Square,
            accum_out=cols[:, 1:2]),
            waits=dsem + [("dve", P.count["dve"]), ("act", P.count["act"])],
            inc="act")
        c3 = P.emit("vector", lambda h, c_mu=c_mu: h.tensor_scalar(
            out=cols[:, c_mu:c_mu + 1], in0=cols[:, 0:1], scalar1=1.0 / E,
            scalar2=None, op0=OP.mult), waits=[("dve", c1)], inc="dve")
        c4 = P.emit("vector", lambda h: h.tensor_scalar(
            out=cols[:, 3:4], in0=cols[:, 1:2], scalar1=1.0 / E, scalar2=None,
            op0=OP.mult), waits=[("act", c2), ("dve", c3)], inc="dve")
        c5 = P.emit("vector", lambda h, c_mu=c_mu: h.tensor_tensor(
            out=cols[:, 4:5], in0=cols[:, c_mu:c_mu + 1],
            in1=cols[:, c_mu:c_mu + 1], op=OP.mult),
            waits=[("dve", c4)], inc="dve")
        c6 = P.emit("vector", lambda h: h.tensor_tensor(
            out=cols[:, 5:6], in0=cols[:, 3:4], in1=cols[:, 4:5], op=OP.subtract),
            waits=[("dve", c5)], inc="dve")
        c7 = P.emit("scalar", lambda h: h.activation(
            out=cols[:, 6:7], in_=cols[:, 5:6], func=AF.Sqrt, bias=eps),
            waits=[("dve", c6)], inc="act")
        c8 = P.emit("vector", lambda h, c_r=c_r: h.reciprocal(
            out=cols[:, c_r:c_r + 1], in_=cols[:, 6:7]),
            waits=[("act", c7)], inc="dve")
        return c8, c2

    for t in range(NTT):
        sl = t % 2
        w = []
        if t >= 2:
            w += [("dve", prev["aff"][t - 2]), ("act", prev["sq"][t - 2])]
        P.dma("sync", xs[sl][:, :], x_dram[t * 128:(t + 1) * 128, :],
              f"{pfx}_x{sl}", waits=w)
        dsem = [(f"{pfx}_x{sl}", P.dcount[f"{pfx}_x{sl}"])]
        c8, csq = stats(xs[sl], 2, 7, dsem, 1e-5)
        a1 = P.emit("vector", lambda h, sl=sl: h.tensor_scalar(
            out=xa[sl][:, :], in0=xs[sl][:, :], scalar1=cols[:, 2:3],
            scalar2=cols[:, 7:8], op0=OP.subtract, op1=OP.mult),
            waits=[("dve", c8)], inc="dve")
        a2 = P.emit("vector", lambda h, sl=sl: h.tensor_copy(
            out=xt[sl][:, :], in_=xa[sl][:, :]), waits=[("dve", a1)], inc="dve")
        c8b, csq2 = stats(xa[sl], 8, 9, [("dve", a1)], 1e-5)
        prev["sq"][t] = csq2
        w2 = [("dve", c8b)]
        if t >= 2:
            w2.append(("pe", prev["tp"][t - 2]))
        a3 = P.emit("vector", lambda h, sl=sl: h.tensor_scalar(
            out=xt2[sl][:, :], in0=xa[sl][:, :], scalar1=cols[:, 8:9],
            scalar2=cols[:, 9:10], op0=OP.subtract, op1=OP.mult),
            waits=w2, inc="dve")
        prev["aff"][t] = a3
        for j in range(8):
            for which, (srct, dsts) in enumerate([(xt[sl], x1tmp[sl]),
                                                  (xt2[sl], None)]):
                bv = bva if which == 0 else bvb
                key = (t, j, which)
                w3 = [("dve", a3 if which == 1 else a2)]
                pk = (t, j - 1, which) if j >= 1 else (t - 1, 7, which)
                if pk in prev["evac"]:
                    w3.append(("act", prev["evac"][pk]))
                cp = P.emit("tensor", lambda h, srct=srct, j=j, bv=bv: h.transpose(
                    out=bv[:, 0:128], in_=srct[:, j * 128:(j + 1) * 128],
                    identity=c["eyeb"][:, :]), waits=w3, inc="pe")
                if which == 0:
                    wq = [("pe", cp), (f"{pfx}_o{t % 2}", prev["gl"].get(t - 2, 0))]
                    if pk in prev["evac"]:
                        wq.append(("act", prev["evac"][pk]))
                    ce = P.emit("scalar", lambda h, j=j, bv=bv, sl=sl:
                                h.activation(out=x1tmp[sl][:, j * 128:(j + 1) * 128],
                                             in_=bv[:, 0:128], func=AF.Copy),
                                waits=wq, inc="act")
                else:
                    ce = P.emit("scalar", lambda h, j=j, t=t, bv=bv:
                                h.activation(out=xnn_tiles[j][:, t * 128:(t + 1) * 128],
                                             in_=bv[:, 0:128], func=AF.Copy),
                                waits=[("pe", cp)], inc="act")
                prev["evac"][key] = ce
        lastev = prev["evac"][(t, 7, 0)]
        for j in range(8):
            P.dma("sync", xn1_scr[t, j], x1tmp[sl][:, j * 128:(j + 1) * 128],
                  f"{pfx}_o{sl}", waits=[("act", lastev)] if j == 0 else [])
        prev["gl"][t] = P.dcount[f"{pfx}_o{sl}"]
        wmm = [("act", lastev), ("act", prev["exp"])]
        for kd in range(8):
            P.emit("tensor", lambda h, kd=kd, sl=sl: h.matmul(
                out=bk_l[:, 0:8], lhsT=x1tmp[sl][:, kd * 128:(kd + 1) * 128],
                rhs=gwb1[kd][:, :], start=(kd == 0), stop=(kd == 7)),
                waits=wmm if kd == 0 else [], inc="pe" if kd == 7 else None)
        pe_n = P.count["pe"]
        v1 = P.emit("vector", lambda h: h.reduce_max(
            out=gc[:, 0:1], in_=bk_l[:, 0:8], axis=mybir.AxisListType.X),
            waits=[("pe", pe_n), ("dve", P.count["dve"])], inc="dve")
        v2 = P.emit("vector", lambda h: h.tensor_scalar(
            out=gc[:, 1:2], in0=gc[:, 0:1], scalar1=-1.0, scalar2=None,
            op0=OP.mult), waits=[("dve", v1)], inc="dve")
        a1g = P.emit("scalar", lambda h: h.activation(
            out=g8[:, 0:8], in_=bk_l[:, 0:8], func=AF.Exp, bias=gc[:, 1:2]),
            waits=[("dve", v2), ("act", prev["exp"])], inc="act")
        prev["exp"] = a1g
        v3 = P.emit("vector", lambda h: h.reduce_sum(
            out=gc[:, 2:3], in_=g8[:, 0:8], axis=mybir.AxisListType.X),
            waits=[("act", a1g), ("dve", P.count["dve"])], inc="dve")
        v4 = P.emit("vector", lambda h: h.reciprocal(
            out=gc[:, 3:4], in_=gc[:, 2:3]), waits=[("dve", v3)], inc="dve")
        v5 = P.emit("vector", lambda h: h.tensor_scalar(
            out=g8[:, 8:16], in0=g8[:, 0:8], scalar1=gc[:, 3:4], scalar2=None,
            op0=OP.mult), waits=[("dve", v4)], inc="dve")
        v6 = P.emit("vector", lambda h: h.reduce_max(
            out=gc[:, 4:5], in_=g8[:, 8:16], axis=mybir.AxisListType.X),
            waits=[("dve", v5)], inc="dve")
        v7 = P.emit("vector", lambda h: h.tensor_scalar(
            out=g8[:, 16:24], in0=g8[:, 8:16], scalar1=gc[:, 4:5], scalar2=None,
            op0=OP.is_equal), waits=[("dve", v6)], inc="dve")
        v8 = P.emit("vector", lambda h: h.scalar_tensor_tensor(
            out=g8[:, 24:32], in0=g8[:, 16:24], scalar=-1e30, in1=g8[:, 8:16],
            op0=OP.mult, op1=OP.add), waits=[("dve", v7)], inc="dve")
        v9 = P.emit("vector", lambda h: h.reduce_max(
            out=gc[:, 5:6], in_=g8[:, 24:32], axis=mybir.AxisListType.X),
            waits=[("dve", v8)], inc="dve")
        v10 = P.emit("vector", lambda h: h.tensor_tensor(
            out=gc[:, 6:7], in0=gc[:, 4:5], in1=gc[:, 5:6], op=OP.add),
            waits=[("dve", v9)], inc="dve")
        P.emit("vector", lambda h, t=t: h.tensor_scalar(
            out=selw1_all[:, t * 8:(t + 1) * 8], in0=g8[:, 8:16],
            scalar1=gc[:, 5:6], scalar2=gc[:, 6:7], op0=OP.is_ge, op1=OP.mult),
            waits=[("dve", v10)], inc="dve")
        prev["tp"][t] = P.count["pe"]
    return bag


L1PARTS = {"within", "cross", "state", "div"}
L1NPAIRS = HP
L1NCH = NTG


def emit_l1_attention(nc, P, c, xnn_tiles, w1qk, v1_scr, y_scr, banks, bag):
    (BA0, BA1, BB0, BB1, BNA, BNB, BST, BBC) = banks
    W1qb, W1kb = w1qk
    qf = _alloc(nc, bag, "l1qf", [128, S], BF16)
    kf = _alloc(nc, bag, "l1kf", [128, S], BF16)
    kfT = _alloc(nc, bag, "l1kfT", [128, S], BF16)
    vp1 = _alloc(nc, bag, "l1vp", [128, S], BF16)
    vaug = [_alloc(nc, bag, f"l1va{i}", [128, 2080], BF16) for i in range(2)]
    rb2 = _alloc(nc, bag, "l1rb2", [128, 512], BF16)
    prr = _alloc(nc, bag, "l1prr", [128, 512], BF16)
    prn = _alloc(nc, bag, "l1prn", [128, 512], BF16)
    pre = _alloc(nc, bag, "l1pre", [128, 512], BF16)
    asb = [_alloc(nc, bag, f"l1as{i}", [128, 512], BF16) for i in range(2)]
    bsb = [_alloc(nc, bag, f"l1bs{i}", [128, 512], BF16) for i in range(2)]
    S_sb = _alloc(nc, bag, "l1Ssb", [128, 128], BF16)
    rrt = _alloc(nc, bag, "l1rrt", [128, 1024], F32)
    rrtb = _alloc(nc, bag, "l1rrtb", [128, 1024], BF16)
    rb = _alloc(nc, bag, "l1rb", [128, 512], BF16)
    ya = _alloc(nc, bag, "l1ya", [128, 512], BF16)
    yb = _alloc(nc, bag, "l1yb", [128, 512], BF16)
    kv = (BNA, BNB)
    bcv2 = [BA0[:, :].bitcast(BF16), BA1[:, :].bitcast(BF16)]
    done = {}
    P.emit("vector", lambda h: h.memset(rrtb[:, :], 0.0),
           waits=[("dve", P.count["dve"])], inc="dve")

    for p in range(L1NPAIRS):
        dep = done.get(p - 1)
        theta = {}
        for which, (wt_t, dst) in enumerate([(W1qb, qf), (W1kb, kf)]):
            for tg in range(NTG):
                wmm = [("act", P.count["act"])]
                if dep and which == 0 and tg == 0:
                    wmm += [("dve", dep["dve"]), ("pe", dep["pe"])]
                for kd in range(8):
                    P.emit("tensor", lambda h, kd=kd, wt_t=wt_t, tg=tg, p=p:
                           h.matmul(out=BNA[:, :],
                                    lhsT=wt_t[kd][:, p * 128:(p + 1) * 128],
                                    rhs=xnn_tiles[kd][:, tg * 512:(tg + 1) * 512],
                                    start=(kd == 0), stop=(kd == 7)),
                           waits=wmm if kd == 0 else [], inc="pe" if kd == 7 else None)
                pe_p = P.count["pe"]
                e1 = P.emit("scalar", lambda h: h.activation(
                    out=prr[:, :], in_=BNA[:, :], func=AF.Relu),
                    waits=[("pe", pe_p), ("act", P.count["act"]),
                           ("dve", P.count["dve"])], inc="act")
                e2 = P.emit("scalar", lambda h: h.activation(
                    out=prn[:, :], in_=BNA[:, :], func=AF.Relu, scale=-1.0),
                    waits=[("act", e1)], inc="act")
                e3 = P.emit("scalar", lambda h: h.activation(
                    out=pre[:, :], in_=prn[:, :], func=AF.Exp, scale=-1.0),
                    waits=[("act", e2)], inc="act")
                wd1 = [("act", e3), ("dve", P.count["dve"])]
                if dep and tg == 0:
                    wd1.append(("pe", dep["pe"]))
                d1 = P.emit("vector", lambda h, dst=dst, tg=tg: h.tensor_tensor(
                    out=dst[:, tg * 512:(tg + 1) * 512], in0=prr[:, :],
                    in1=pre[:, :], op=OP.add),
                    waits=wd1, inc="dve")
                theta[(which, tg)] = d1
        kdone = max(theta.values())
        tev = {}
        ktdone = kdone
        for i in range(NTT if "kft" in L1PARTS or "state" in L1PARTS else 0):
            tsl = i % 2
            w3 = [("dve", kdone)]
            if i >= 2:
                w3.append(("dve", tev[i - 2]))
            tp1 = P.emit("tensor", lambda h, i=i, tsl=tsl: h.transpose(
                out=bcv2[tsl][:, 0:128],
                in_=kf[:, i * 128:(i + 1) * 128],
                identity=c["eyeb"][:, :]), waits=w3, inc="pe")
            tev[i] = P.emit("vector", lambda h, i=i, tsl=tsl: h.tensor_copy(
                out=kfT[:, i * 128:(i + 1) * 128],
                in_=bcv2[tsl][:, 0:128]),
                waits=[("pe", tp1), ("dve", P.count["dve"])], inc="dve")
            ktdone = tev[i]
        wv = [("pe", P.count["pe"]),
              ("v1p_s0", P.dcount.get("v1p_s0", 0)),
              ("v1p_s1", P.dcount.get("v1p_s1", 0))]
        if dep:
            wv.append(("dve", dep["dve"]))
        for i in range(4 if ("vload" in L1PARTS or "within" in L1PARTS
                             or "state" in L1PARTS) else 0):
            P.dma("sync", vp1[:, i * 1024:(i + 1) * 1024].rearrange(
                      "q (t d) -> q t d", d=128),
                  v1_scr[p, i * 1024:(i + 1) * 1024, :].rearrange(
                      "(t q) d -> q t d", q=128),
                  f"l1v{p % 2}", waits=wv if i == 0 else [])
        vwait = (f"l1v{p % 2}", P.dcount.get(f"l1v{p % 2}", 0))
        vw0 = [vwait, ("dve", P.count["dve"]), ("pe", P.count["pe"])]
        va_last = 0
        for kg in range(NTT):
            for X in range(2):
                cc = P.emit("vector", lambda h, kg=kg, X=X: h.tensor_copy(
                    out=vaug[X][:, kg * 65:kg * 65 + 64],
                    in_=vp1[:, kg * 128 + 64 * X:kg * 128 + 64 * X + 64]),
                    waits=vw0 if (kg == 0 and X == 0) else [], inc="dve")
                va_last = P.emit("vector", lambda h, kg=kg, X=X: h.memset(
                    vaug[X][:, kg * 65 + 64:kg * 65 + 65], 1.0),
                    waits=[("dve", cc)], inc="dve")
        vwait = ("dve", va_last)
        sprev = 0
        ydone = 0
        ev_prev = {}
        st_first = True
        for ch in range(L1NCH):
            chs = slice(ch * 512, (ch + 1) * 512)
            nfirst = {0: True, 1: True}
            if ch > 0 and "cross" in L1PARTS:
                cw = [("dve", sprev), ("dve", ydone), ("act", P.count["act"])]
                P.emit("tensor", lambda h, chs=chs: h.matmul(
                    out=BNA[0:65, :], lhsT=S_sb[0:64, 0:65], rhs=qf[0:64, chs],
                    start=True, stop=False, tile_position=(0, 0)),
                    waits=cw, inc="pe")
                P.emit("tensor", lambda h, chs=chs: h.matmul(
                    out=BNB[0:65, :], lhsT=S_sb[64:128, 0:65], rhs=qf[64:128, chs],
                    start=True, stop=False, tile_position=(64, 0)),
                    waits=[], inc="pe")
                nfirst = {0: False, 1: False}
            for kci in range(4 if "within" in L1PARTS else 0):
                par = kci % 2
                kg = ch * 4 + kci
                zlo = kci * 128
                for X in range(2):
                    bank = (BA0, BA1)[par] if X == 0 else (BB0, BB1)[par]
                    sgn = (asb, bsb)[X][par]
                    wq = []
                    if (X, par) in ev_prev:
                        wq.append(ev_prev[(X, par)])
                    if ch == 0 and kci in (0, 1):
                        wq.append(("dve", kdone))
                        wq.append(("dve", ktdone))
                    q1 = P.emit("tensor", lambda h, X=X, kg=kg, chs=chs, bank=bank:
                                h.matmul(out=bank[:, :],
                                         lhsT=kf[64 * X:64 * X + 64,
                                                 kg * 128:(kg + 1) * 128],
                                         rhs=qf[64 * X:64 * X + 64, chs],
                                         start=True, stop=True,
                                         tile_position=(64 * X, 0)),
                                waits=wq, inc="pe")
                    if X == 0:
                        e1 = P.emit("scalar", lambda h, bank=bank, sgn=sgn:
                                    h.activation(out=sgn[:, :], in_=bank[:, :],
                                                 func=AF.Copy),
                                    waits=[("pe", q1), ("act", P.count["act"]),
                                           ("dve", P.count["dve"])], inc="act")
                        eeng = "act"
                    else:
                        e1 = P.emit("vector", lambda h, bank=bank, sgn=sgn:
                                    h.tensor_copy(out=sgn[:, :], in_=bank[:, :]),
                                    waits=[("pe", q1), ("dve", P.count["dve"]),
                                           ("act", P.count["act"])], inc="dve")
                        eeng = "dve"
                    fix = e1
                    if zlo > 0:
                        fix = P.emit("vector", lambda h, sgn=sgn, zlo=zlo: h.memset(
                            sgn[:, 0:zlo], 0.0), waits=[(eeng, e1)], inc="dve")
                    fix = P.emit("vector", lambda h, sgn=sgn, zlo=zlo:
                                 h.tensor_tensor(out=sgn[:, zlo:zlo + 128],
                                                 in0=sgn[:, zlo:zlo + 128],
                                                 in1=c["tril"][:, :], op=OP.mult),
                                 waits=[(eeng, e1) if fix == e1 else ("dve", fix),
                                        ("dve", P.count["dve"])], inc="dve")
                    wn = [("dve", fix)]
                    if nfirst[X]:
                        wn.append(("dve", P.count["dve"]))
                        wn.append(("act", P.count["act"]))
                        wn.append(vwait)
                    pnx = BNA if X == 0 else BNB
                    n1 = P.emit("tensor", lambda h, X=X, kg=kg, sgn=sgn, pnx=pnx,
                                st=nfirst[X], lstn=(kci == 3):
                                h.matmul(out=pnx[0:65, :],
                                         lhsT=vaug[X][:, kg * 65:(kg + 1) * 65],
                                         rhs=sgn[:, :], start=st, stop=lstn,
                                         tile_position=(0, 0)),
                                waits=wn, inc="pe")
                    nfirst[X] = False
                    ev_prev[(X, par)] = (eeng, e1)
            for kci in range(4 if "state" in L1PARTS else 0):
                kg = ch * 4 + kci
                ksl = slice(kg * 128, (kg + 1) * 128)
                lstS = (ch == NTG - 1 and kci == 3)
                wst = [("dve", ktdone)]
                if st_first:
                    wst.append(("dve", P.count["dve"]))
                P.emit("tensor", lambda h, ksl=ksl, st=st_first: h.matmul(
                    out=BST[0:64, 0:64], lhsT=kfT[:, ksl][:, 0:64],
                    rhs=vp1[:, ksl][:, 0:64], start=st, stop=False,
                    tile_position=(0, 0), skip_group_check=True),
                    waits=wst, inc="pe")
                P.emit("tensor", lambda h, ksl=ksl, st=st_first: h.matmul(
                    out=BST[64:128, 0:64], lhsT=kfT[:, ksl][:, 64:128],
                    rhs=vp1[:, ksl][:, 64:128], start=st, stop=False,
                    tile_position=(0, 64), skip_group_check=True),
                    waits=[], inc="pe")
                P.emit("tensor", lambda h, ksl=ksl, st=st_first: h.matmul(
                    out=BST[0:64, 64:65], lhsT=kfT[:, ksl][:, 0:64],
                    rhs=c["ones_col"][:, :], start=st, stop=False,
                    tile_position=(0, 0), skip_group_check=True),
                    waits=[], inc="pe")
                P.emit("tensor", lambda h, ksl=ksl, st=st_first, lstS=lstS: h.matmul(
                    out=BST[64:128, 64:65], lhsT=kfT[:, ksl][:, 64:128],
                    rhs=c["ones_col"][:, :], start=st, stop=lstS,
                    tile_position=(0, 64), skip_group_check=True),
                    waits=[], inc="pe")
                st_first = False
            pe_st = P.count["pe"]
            if "state" in L1PARTS:
                sprev = P.emit("vector", lambda h: h.tensor_copy(
                    out=S_sb[:, 0:65], in_=BST[:, 0:65]),
                    waits=[("pe", pe_st), ("dve", P.count["dve"])], inc="dve")
            if "div" not in L1PARTS and "div1" not in L1PARTS and \
               "div2" not in L1PARTS and "div3" not in L1PARTS:
                continue
            r1 = P.emit("vector", lambda h: h.reciprocal(
                out=rrt[64:65, 0:512], in_=BNA[64:65, :]),
                waits=[("pe", pe_st), ("dve", P.count["dve"])], inc="dve")
            r2 = P.emit("vector", lambda h: h.reciprocal(
                out=rrt[64:65, 512:1024], in_=BNB[64:65, :]),
                waits=[("dve", r1)], inc="dve")
            if "div1" in L1PARTS:
                ydone = r2
                continue
            rc1 = P.emit("vector", lambda h: h.tensor_copy(
                out=rrtb[64:65, 0:512], in_=rrt[64:65, 0:512]),
                waits=[("dve", r2)], inc="dve")
            rc2 = P.emit("vector", lambda h: h.tensor_copy(
                out=rrtb[64:65, 512:1024], in_=rrt[64:65, 512:1024]),
                waits=[("dve", rc1)], inc="dve")
            b1 = P.emit("tensor", lambda h: h.matmul(
                out=BBC[0:64, :], lhsT=c["bc128"][:, 0:64],
                rhs=rrtb[:, 0:512], start=True, stop=True),
                waits=[("dve", rc2)], inc="pe")
            b2 = P.emit("tensor", lambda h: h.matmul(
                out=BA0[0:64, :], lhsT=c["bc128"][:, 0:64],
                rhs=rrtb[:, 512:1024], start=True, stop=True),
                waits=[], inc="pe")
            rbv = P.emit("vector", lambda h: h.tensor_copy(
                out=rb[0:64, :], in_=BBC[0:64, :]),
                waits=[("pe", b2), ("dve", P.count["dve"])], inc="dve")
            rbv2 = P.emit("vector", lambda h: h.tensor_copy(
                out=rb2[0:64, :], in_=BA0[0:64, :]),
                waits=[("dve", rbv)], inc="dve")
            y1 = P.emit("vector", lambda h: h.tensor_tensor(
                out=ya[0:64, :], in0=BNA[0:64, :], in1=rb[0:64, :], op=OP.mult),
                waits=[("dve", rbv2), ("l1yo", P.dcount.get("l1yo", 0))],
                inc="dve")
            y2 = P.emit("vector", lambda h: h.tensor_tensor(
                out=yb[0:64, :], in0=BNB[0:64, :], in1=rb2[0:64, :],
                op=OP.mult), waits=[("dve", y1)], inc="dve")
            if "div3" in L1PARTS:
                ydone = y2
                continue
            P.dma("sync", y_scr[p, 0:64, ch * 512:(ch + 1) * 512], ya[0:64, :],
                  "l1yo", waits=[("dve", y1)])
            P.dma("sync", y_scr[p, 64:128, ch * 512:(ch + 1) * 512], yb[0:64, :],
                  "l1yo", waits=[("dve", y2)])
            ydone = y2
        done[p] = {"pe": P.count["pe"], "dve": P.count["dve"],
                   "act": P.count["act"]}


def emit_l1_w2proj(nc, P, c, W2b, y_scr, s1_scr, banks, bag):
    (B0, B1, BT, BT2) = banks
    btv = [BT[:, :].bitcast(BF16), BT2[:, :].bitcast(BF16)]
    yin = [_alloc(nc, bag, f"w2y{i}", [128, 4096], BF16) for i in range(2)]
    ssb = _alloc(nc, bag, "w2ssb", [128, 512], BF16)
    so = [_alloc(nc, bag, f"w2so{i}", [128, 128], BF16) for i in range(2)]
    prev_ev = {}
    prev_tr = {}
    pe_done = {}
    for tg in range(NTG):
        ysl = tg % 2
        wv = [("pe", pe_done.get(tg - 2, 0)), ("l1yo", P.dcount.get("l1yo", 0))]
        for kd in range(8):
            P.dma("sync", yin[ysl][:, kd * 512:(kd + 1) * 512],
                  y_scr[kd, :, tg * 512:(tg + 1) * 512], f"w2y{ysl}",
                  waits=wv if kd == 0 else [])
        yw = (f"w2y{ysl}", P.dcount[f"w2y{ysl}"])
        for do in range(8):
            bank = (B0, B1)[do % 2]
            wmm = [yw]
            if do >= 2:
                wmm.append(("act", prev_ev[(tg, do - 2)]))
            elif tg > 0:
                wmm.append(("act", prev_ev[(tg - 1, 6 + do)]))
            for kd in range(8):
                P.emit("tensor", lambda h, kd=kd, do=do, ysl=ysl, bank=bank:
                       h.matmul(out=bank[:, :],
                                lhsT=W2b[kd][:, do * 128:(do + 1) * 128],
                                rhs=yin[ysl][:, kd * 512:(kd + 1) * 512],
                                start=(kd == 0), stop=(kd == 7)),
                       waits=wmm if kd == 0 else [], inc="pe" if kd == 7 else None)
            pe_n = P.count["pe"]
            ev = P.emit("scalar", lambda h, bank=bank: h.activation(
                out=ssb[:, :], in_=bank[:, :], func=AF.Copy),
                waits=[("pe", pe_n), ("act", P.count["act"]),
                       ("pe", P.count["pe"])], inc="act")
            prev_ev[(tg, do)] = ev
            for sb in range(4):
                tsl = sb % 2
                w3 = [("act", ev)]
                pk = prev_tr.get((tg, do, sb - 2) if sb >= 2 else "last")
                if pk:
                    w3.append(("act", pk))
                t1 = P.emit("tensor", lambda h, sb=sb, tsl=tsl: h.transpose(
                    out=btv[tsl][:, 0:128],
                    in_=ssb[:, sb * 128:(sb + 1) * 128],
                    identity=c["eyeb"][:, :]), waits=w3, inc="pe")
                ssl = sb % 2
                e2 = P.emit("scalar", lambda h, tsl=tsl, ssl=ssl: h.activation(
                    out=so[ssl][:, :], in_=btv[tsl][:, 0:128],
                    func=AF.Copy),
                    waits=[("pe", t1),
                           (f"s1o{ssl}", P.dcount.get(f"s1o{ssl}", 0))], inc="act")
                P.dma("sync",
                      s1_scr[do, (tg * 4 + sb) * 128:(tg * 4 + sb + 1) * 128, :],
                      so[ssl][:, :], f"s1o{ssl}", waits=[("act", e2)])
                prev_tr[(tg, do, sb)] = e2
                prev_tr["last"] = e2
            prev_ev[(tg, do)] = prev_tr["last"]
        pe_done[tg] = P.count["pe"]


def emit_l1_moe_final(nc, P, c, xn1_scr, x1_dram, out_dram, w1b_scr, w2b_scr,
                      selw_all, s1_scr, banks, bag):
    w1buf = [_alloc(nc, bag, f"m1w1_{i}", [128, 4096], BF16) for i in range(8)]
    w2rot = [_alloc(nc, bag, f"m1w2_{i}", [128, 1024], BF16) for i in range(4)]
    xnb = [_alloc(nc, bag, f"m1xn_{i}", [128, 4096], BF16) for i in range(2)]
    hall = [_alloc(nc, bag, f"m1h_{i}", [128, 16384], BF16) for i in range(1)]
    yacc = [_alloc(nc, bag, f"m1y_{i}", [128, 1024], F32) for i in range(8)]
    g1 = _alloc(nc, bag, "m1g1", [128, 1024], BF16)
    s1i = [_alloc(nc, bag, f"m1s_{i}", [128, 1024], BF16) for i in range(2)]
    x1i = [_alloc(nc, bag, f"m1x_{i}", [128, 1024], F32) for i in range(2)]
    oti = [_alloc(nc, bag, f"m1o_{i}", [128, 1024], F32) for i in range(2)]
    hall = hall[0]
    last = {}
    for Q in range(4):
        z = []
        for i in range(8):
            z.append(P.emit("vector", lambda h, i=i: h.memset(yacc[i][:, :], 0.0),
                            waits=[("dve", last.get("yread", 0)),
                                   ("act", P.count["act"])] if i == 0 else [],
                            inc="dve"))
        zdone = z[-1]
        for e in range(NEXP):
            ww = [("pe", last.get("w1", 0)), ("gps", P.count["gps"]),
                  ("gco0", P.dcount.get("gco0", 0)), ("gco1", P.dcount.get("gco1", 0))]
            for kd in range(8):
                P.dma("sync", w1buf[kd][:, :], w1b_scr[e, kd * 128:(kd + 1) * 128, :],
                      "m1w1", waits=ww if kd == 0 else [])
            w1w = ("m1w1", P.dcount["m1w1"])
            for tg in range(2):
                tgg = Q * 2 + tg
                xsl = tg % 2
                wx = [("pe", last.get(("xn", xsl), 0)),
                      ("ln1_o0", P.dcount.get("ln1_o0", 0)),
                      ("ln1_o1", P.dcount.get("ln1_o1", 0))]
                for kd in range(8):
                    P.dma("sync", xnb[xsl][:, kd * 512:(kd + 1) * 512].rearrange(
                              "q (t c) -> q t c", c=128),
                          xn1_scr[tgg * 4:(tgg + 1) * 4, kd].rearrange(
                              "t q c -> q t c"),
                          f"m1x{xsl}", waits=wx if kd == 0 else [])
                xw = (f"m1x{xsl}", P.dcount[f"m1x{xsl}"])
                hevs = {}
                for hc in range(32):
                    bank = banks[hc % 2]
                    wmm = [w1w, xw, ("act", hevs.get(hc - 2, last.get("h", 0)))]
                    if hc == 0:
                        wmm.append(("dve", last.get("yadd", 0)))
                    for kd in range(8):
                        P.emit("tensor", lambda h, kd=kd, hc=hc, xsl=xsl, bank=bank:
                               h.matmul(out=bank[:, :],
                                        lhsT=w1buf[kd][:, hc * 128:(hc + 1) * 128],
                                        rhs=xnb[xsl][:, kd * 512:(kd + 1) * 512],
                                        start=(kd == 0), stop=(kd == 7)),
                               waits=wmm if kd == 0 else [],
                               inc="pe" if kd == 7 else None)
                    pe_h = P.count["pe"]
                    hevs[hc] = P.emit("scalar", lambda h, hc=hc, bank=bank:
                                      h.activation(out=hall[:, hc * 512:(hc + 1) * 512],
                                                   in_=bank[:, :], func=AF.Gelu),
                                      waits=[("pe", pe_h), ("act", P.count["act"])],
                                      inc="act")
                last["h"] = hevs[31]
                last["w1read"] = P.count["pe"]
                hwaits = hevs
                for hc in range(32):
                    wsl = hc % 4
                    ww2 = [("pe", last.get(("w2", wsl), 0)),
                           ("gco0", P.dcount.get("gco0", 0)),
                           ("gco1", P.dcount.get("gco1", 0))]
                    P.dma("sync", w2rot[wsl][:, :],
                          w2b_scr[e, hc * 128:(hc + 1) * 128, :], f"m1w2_{wsl}",
                          waits=ww2)
                    w2w = (f"m1w2_{wsl}", P.dcount[f"m1w2_{wsl}"])
                    for tt in range(4):
                        for half in range(2):
                            bidx = tt * 2 + half
                            wmm = [w2w, ("act", hwaits[hc])]
                            if hc == 0:
                                wmm.append(("act", last["h"]))
                                wmm.append(("dve", zdone if e == 0 and tg == 0
                                            else last.get("yadd", 0)))
                            P.emit("tensor", lambda h, hc=hc, tt=tt, half=half,
                                   wsl=wsl, bidx=bidx:
                                   h.matmul(out=banks[bidx][:, :],
                                            lhsT=hall[:, hc * 512 + tt * 128:
                                                      hc * 512 + (tt + 1) * 128],
                                            rhs=w2rot[wsl][:, half * 512:
                                                           (half + 1) * 512],
                                            start=(hc == 0), stop=(hc == 31)),
                                   waits=wmm if (tt == 0 and half == 0) else [],
                                   inc="pe" if (tt == 3 and half == 1) else None)
                    last[("w2", wsl)] = P.count["pe"]
                pe_y = P.count["pe"]
                ya = 0
                for tt in range(4):
                    for half in range(2):
                        bidx = tt * 2 + half
                        tglob = tgg * 4 + tt
                        ya = P.emit("vector", lambda h, tt=tt, half=half, e=e,
                                    tglob=tglob, tg=tg, bidx=bidx:
                                    h.scalar_tensor_tensor(
                                        out=yacc[tg * 4 + tt][:, half * 512:
                                                              (half + 1) * 512],
                                        in0=banks[bidx][:, :],
                                        scalar=selw_all[:, tglob * 8 + e:
                                                        tglob * 8 + e + 1],
                                        in1=yacc[tg * 4 + tt][:, half * 512:
                                                              (half + 1) * 512],
                                        op0=OP.mult, op1=OP.add),
                                    waits=[("pe", pe_y), ("dve", P.count["dve"])],
                                    inc="dve")
                last["yadd"] = ya
                last[("xn", xsl)] = P.count["pe"]
            last["w1"] = last["w1read"]
        for tt8 in range(8):
            tglob = Q * 8 + tt8
            sl = tt8 % 2
            wld = [("dve", last.get(("cmb", sl), 0))]
            P.dma("sync", x1i[sl][:, :], x1_dram[tglob * 128:(tglob + 1) * 128, :],
                  f"m1cx{sl}", waits=wld)
            for pp in range(HP):
                ws1 = (wld + [("s1o0", P.dcount.get("s1o0", 0)),
                              ("s1o1", P.dcount.get("s1o1", 0))]) if pp == 0 else []
                P.dma("sync", s1i[sl][:, pp * 128:(pp + 1) * 128],
                      s1_scr[pp, tglob * 128:(tglob + 1) * 128, :], f"m1cs{sl}",
                      waits=ws1)
            a1 = P.emit("scalar", lambda h, tt8=tt8: h.activation(
                out=g1[:, :], in_=yacc[tt8][:, :], func=AF.Silu),
                waits=[("dve", P.count["dve"]), ("act", P.count["act"])], inc="act")
            c1 = P.emit("vector", lambda h, sl=sl: h.tensor_tensor(
                out=s1i[sl][:, :], in0=s1i[sl][:, :], in1=g1[:, :], op=OP.mult),
                waits=[("act", a1), (f"m1cs{sl}", P.dcount[f"m1cs{sl}"]),
                       ("dve", P.count["dve"])], inc="dve")
            c2 = P.emit("vector", lambda h, sl=sl: h.tensor_tensor(
                out=oti[sl][:, :], in0=x1i[sl][:, :], in1=s1i[sl][:, :],
                op=OP.add),
                waits=[("dve", c1), (f"m1cx{sl}", P.dcount[f"m1cx{sl}"]),
                       (f"m1oo{sl}", P.dcount.get(f"m1oo{sl}", 0))], inc="dve")
            P.dma("sync", out_dram[tglob * 128:(tglob + 1) * 128, :],
                  oti[sl][:, :], f"m1oo{sl}", waits=[("dve", c2)])
            last[("cmb", sl)] = c2
        last["yread"] = P.count["dve"]



def build_layer1(nc, x1_src=None, out_name="out", pfx="", dbg=None, P=None,
                 banks=None, moe_scr=None, caster=None, router_sb=None):
    own = P is None
    if own:
        P = Plan(nc)
        for s in ["pe", "dve", "act", "gps"]:
            P.sem(s)
        c = _consts(nc, P)
        ConstDB(nc, P, [1e-5], [-1.0, 1.0])
    else:
        c = P._c
    if x1_src is None:
        x1_d = nc.dram_tensor("x1in", [S, E], F32, kind="ExternalInput").ap()
    else:
        x1_d = x1_src
    out_d = nc.dram_tensor(out_name, [S, E], F32, kind="ExternalOutput").ap()
    W1T_d = nc.dram_tensor("W1T", [E, 3 * E], F32, kind="ExternalInput").ap()
    W2T_d = nc.dram_tensor("W2T", [E, E], F32, kind="ExternalInput").ap()
    rw_d = nc.dram_tensor("router_wT", [E, 8], F32, kind="ExternalInput").ap()
    if moe_scr is None:
        mw1_d = nc.dram_tensor("m1_w1T", [NEXP, E, 4 * E], F32, kind="ExternalInput").ap()
        mw2_d = nc.dram_tensor("m1_w2T", [NEXP, 4 * E, E], F32, kind="ExternalInput").ap()

    kv = "ExternalOutput" if dbg in ("l1y",) else "Internal"
    v1_scr = nc.dram_tensor("v1_scr", [HP, S, 128], BF16, kind="Internal").ap()
    y_scr = nc.dram_tensor("y_scr", [HP, 128, S], BF16, kind=kv).ap()
    s1_scr = nc.dram_tensor("s1_scr", [HP, S, 128], BF16, kind="Internal").ap()
    xn1_scr = nc.dram_tensor("xn1_scr", [NTT, 8, 128, 128], BF16, kind="Internal").ap()
    if moe_scr is None:
        w1b1_scr = nc.dram_tensor("w1b1_scr", [NEXP, E, 4 * E], BF16, kind="Internal").ap()
        w2b1_scr = nc.dram_tensor("w2b1_scr", [NEXP, 4 * E, E], BF16, kind="Internal").ap()
    else:
        w1b1_scr, w2b1_scr = moe_scr

    if banks is None:
        banks = [nc.psum_tensor(f"p1b{i}", [128, 512], F32).__enter__() for i in range(8)]

    if router_sb is None:
        selw1 = nc.sbuf_tensor("selw1", [128, NTT * 8], F32).__enter__()
        gwb1 = [nc.sbuf_tensor(f"gwb1_{i}", [128, 8], BF16).__enter__()
                for i in range(8)]
    else:
        selw1, gwb1 = router_sb

    if caster is not None:
        casts, cstate, fused_castbag = caster
        cstate["extra"] = [(s, P.dcount.get(s, 0))
                           for s in ("x1o0", "x1o1") if P.dcount.get(s, 0) > 0]
        cstate["extra_uses"] = 2
        for go in casts:
            go()
        casts = []
        castbag = []
    elif moe_scr is None:
        jobs = []
        for e in range(NEXP):
            jobs.append((mw1_d[e], w1b1_scr[e]))
        for e in range(NEXP):
            jobs.append((mw2_d[e], w2b1_scr[e]))
        casts, castbag, _cst = make_gps_casts(nc, P, jobs, pfx="g1")
        fused_castbag = None
    else:
        casts, castbag = [], []
        fused_castbag = None
    stgbag = []
    stage = [_alloc(nc, stgbag, f"l1st{i}", [128, E], F32) for i in range(2)]
    stage2 = [_alloc(nc, stgbag, f"l1s2{i}", [128, E], F32) for i in range(2)]
    xnnbag = []
    xnn = [_alloc(nc, xnnbag, f"xnn_{i}", [128, S], BF16) for i in range(8)]
    cast_i = [0]

    def drain(n):
        k = 0
        while cast_i[0] < len(casts) and k < n:
            casts[cast_i[0]]()
            cast_i[0] += 1
            k += 1

    if own:
        P.barrier(extra_dma=("const",))
        P.init_psum(banks, c["eyeb"], "const")
    for i in range(8):
        P.dma("sync", stage[0][:, 0:8], rw_d[i * 128:(i + 1) * 128, :], "crw",
              waits=[("dve", P.count["dve"])])
        P.emit("vector", lambda h, i=i: h.tensor_copy(
            out=gwb1[i][:, :], in_=stage[0][:, 0:8]),
            waits=[("crw", P.dcount["crw"])], inc="dve")
    drain(24)

    lnbag = emit_ln2_feature(nc, P, c, x1_d, xnn, xn1_scr, gwb1, selw1,
                             (banks[0], banks[1], banks[2]), pfx + "ln1")
    P.barrier()
    _free(lnbag)
    drain(200)
    if dbg == "l1ln":
        dbg_d = nc.dram_tensor("dbg", [128, NTT * 8], F32, kind="ExternalOutput").ap()
        P.dma("gpsimd", dbg_d, selw1[:, :], "dbgo", waits=[])
        P.finalize()
        return

    wbag = []
    W1vb = [_alloc(nc, wbag, f"W1v{i}", [128, E], BF16) for i in range(8)]
    cast_weight_tiles(nc, P, W1T_d[:, 2 * E:3 * E], W1vb, "vector", "cw1v", stage)
    P.barrier()
    vb = emit_tokproj_spill(nc, P, c, xnn, W1vb, v1_scr, (banks[0], banks[1]),
                            "v1p")
    P.barrier(extra_dma=("v1p_s0", "v1p_s1"))
    _free(vb)
    _free(wbag)
    drain(160)
    if dbg == "l1v":
        P.finalize()
        return

    wbag2 = []
    W1qb = [_alloc(nc, wbag2, f"W1q{i}", [128, E], BF16) for i in range(8)]
    W1kb = [_alloc(nc, wbag2, f"W1k{i}", [128, E], BF16) for i in range(8)]
    cast_weight_tiles(nc, P, W1T_d[:, 0:E], W1qb, "vector", "cw1q", stage)
    cast_weight_tiles(nc, P, W1T_d[:, E:2 * E], W1kb, "scalar", "cw1k", stage2)
    P.barrier()
    drain(len(casts))
    abag = []
    emit_l1_attention(nc, P, c, xnn, (W1qb, W1kb), v1_scr, y_scr, banks, abag)
    P.barrier(extra_dma=("l1yo",))
    _free(abag)
    _free(wbag2)
    _free(xnnbag)
    drain(len(casts))

    if dbg == "l1y":
        P.finalize()
        return

    w2bag = []
    W2b = [_alloc(nc, w2bag, f"W2b{i}", [128, E], BF16) for i in range(8)]
    cast_weight_tiles(nc, P, W2T_d, W2b, "vector", "cw2", stage)
    P.barrier()
    emit_l1_w2proj(nc, P, c, W2b, y_scr, s1_scr,
                   (banks[0], banks[1], banks[2], banks[3]), w2bag)
    P.barrier(extra_dma=("s1o0", "s1o1"))
    _free(w2bag)
    _free(stgbag)
    _free(castbag)
    if fused_castbag:
        cw = [("gps", P.count["gps"]),
              ("gco0", P.dcount.get("gco0", 0)),
              ("gco1", P.dcount.get("gco1", 0))]
        cw = [w for w in cw if w[1] > 0]
        for eng in ["tensor", "vector", "scalar", "sync"]:
            P.emit(eng, lambda h: h.nop(), waits=cw)
        _free(fused_castbag)

    mbag = []
    emit_l1_moe_final(nc, P, c, xn1_scr, x1_d, out_d, w1b1_scr, w2b1_scr,
                      selw1, s1_scr, banks, mbag)
    _free(mbag)
    if own:
        P.finalize()


def _layer1_inputs(inputs, b, tab, x1):
    return {
        "x1in": np.ascontiguousarray(x1),
        "W1T": np.ascontiguousarray(inputs["L1_attn_W1"].T),
        "W2T": np.ascontiguousarray(inputs["L1_attn_W2"].T),
        "router_wT": np.ascontiguousarray(inputs["L1_ffn_router_w"].T),
        "m1_w1T": np.ascontiguousarray(inputs["L1_ffn_w1"].transpose(0, 2, 1)),
        "m1_w2T": np.ascontiguousarray(inputs["L1_ffn_w2"].transpose(0, 2, 1)),
        "eyeb": tab["eyeb"], "eyef": tab["eyef"], "rotT": tab["rotT"],
        "tril": tab["tril"],
        "ones_col": tab["ones_col"], "onesA_col": tab["onesA_col"],
        "onesB_col": tab["onesB_col"], "ones_row": tab["ones_row"],
        "onesA_row": tab["onesA_row"], "onesB_row": tab["onesB_row"],
        "mixrow": tab["mixrow"], "mixrowf": tab["mixrowf"], "bc128": tab["bc128"],
        "cdbf32": _cdb_host_arrays([1e-5], [-1.0, 1.0])[0],
        "cdbbf16": _cdb_host_arrays([1e-5], [-1.0, 1.0])[1],
    }
